# version 30
# baseline (speedup 1.0000x reference)
import numpy as np
import concourse.bass as bass
import concourse.mybir as mybir
from concourse.bass_utils import run_bass_kernel_spmd

F32 = mybir.dt.float32
F32R = mybir.dt.float32r
BF16 = mybir.dt.bfloat16
AF = mybir.ActivationFunctionType
ALU = mybir.AluOpType

SEQ = 4096
DM = 1024
DC = 8
NT = 8
WIN = 512
NW = 8
HEADS = 8
DH = 64
INNER = 512
HID = 2730
HCH = 22
HLAST = 42
EPS = 1e-5

ENGS = ("tensor", "vector", "scalar", "gpsimd", "sync")
DMACH = ("sync", "scalar", "gpsimd")


class Prog:
    def __init__(self):
        self.items = {e: [] for e in ENGS}
        self.cnt = {}
        for e in ENGS:
            self.cnt[("E", e)] = 0
        self.dma_sems = set()
        self.res = {}
        self.wm = {e: {} for e in ENGS}
        self.pending_seed = []

    def _wait_list(self, eng, toks):
        merged = {}
        for tok in toks:
            if tok is None:
                continue
            key, val = tok
            if key == ("E", eng) and eng == "tensor":
                continue
            if self.wm[eng].get(key, 0) >= val:
                continue
            merged[key] = max(merged.get(key, 0), val)
        out = []
        for key, val in merged.items():
            out.append((key, val))
            self.wm[eng][key] = val
        return out

    def emit(self, eng, fn, reads=(), writes=(), dma=False, dmasem=None, tag=""):
        toks = []
        for r in reads:
            rec = self.res.get(r)
            if rec and rec["w"]:
                toks.append(rec["w"])
        for w in writes:
            rec = self.res.get(w)
            if rec:
                if rec["w"]:
                    toks.append(rec["w"])
                toks.extend(rec["r"])
        waits = self._wait_list(eng, toks)
        if dma:
            assert dmasem is not None
            key = ("S", dmasem)
            self.dma_sems.add(dmasem)
            self.cnt[key] = self.cnt.get(key, 0) + 16
            tok = (key, self.cnt[key])
            inc_key = key
        else:
            self.cnt[("E", eng)] += 1
            tok = (("E", eng), self.cnt[("E", eng)])
            inc_key = ("E", eng)
        self.items[eng].append((waits, fn, inc_key, tag))
        for w in writes:
            self.res[w] = {"w": tok, "r": []}
        for r in reads:
            rec = self.res.setdefault(r, {"w": None, "r": []})
            rec["r"].append(tok)
        return tok

    def fin_group(self, dmasem, keys):
        total = self.cnt.get(("S", dmasem), 0)
        for k in keys:
            rec = self.res.get(k)
            if rec and rec["w"] and rec["w"][0] == ("S", dmasem):
                rec["w"] = (("S", dmasem), total)

    def fin_readers(self, dmasem, keys):
        total = self.cnt.get(("S", dmasem), 0)
        for k in keys:
            rec = self.res.get(k)
            if rec:
                rec["r"] = [(t if t[0] != ("S", dmasem) else (("S", dmasem), total))
                            for t in rec["r"]]

    def seed(self, key, toks):
        rec = self.res.setdefault(key, {"w": None, "r": []})
        rec["r"] = rec["r"] + [t for t in toks if t is not None]

    def last_tokens(self, keys):
        out = []
        for k in keys:
            rec = self.res.get(k)
            if rec:
                if rec["w"]:
                    out.append(rec["w"])
                out.extend(rec["r"])
        return out

    def replay_one(self, ename, eng, sems):
        for waits, fn, inc_key, tag in self.items[ename]:
            for key, val in waits:
                eng.wait_ge(sems[key], val)
            inst = fn(eng)
            if inst is None:
                raise RuntimeError(f"op '{tag}' on {ename} returned no instruction")
            inst.then_inc(sems[inc_key], 16 if inc_key[0] == "S" else 1)


def host_consts():
    c = {}
    c["IDENT"] = np.eye(128, dtype=np.float32)
    R64 = np.zeros((64, 64), dtype=np.float32)
    for d in range(32):
        R64[d + 32, d] = -1.0
        R64[d, d + 32] = 1.0
    R = np.zeros((128, 128), dtype=np.float32)
    R[:64, :64] = R64
    R[64:, 64:] = R64
    c["R128"] = R
    c["ONESC"] = np.ones((128, 1), dtype=np.float32)
    c["ONESR"] = np.ones((1, 128), dtype=np.float32)
    c["ONESCR"] = np.ones((128, 1), dtype=np.float32)
    c["ONESRR"] = np.ones((1, 128), dtype=np.float32)
    E = np.zeros((128, 4, 8), dtype=np.float32)
    for mc in range(4):
        E[0:64, mc, 2 * mc] = 1.0
        E[64:128, mc, 2 * mc + 1] = 1.0
    c["EBLK"] = np.ascontiguousarray(E.reshape(128, 32))
    Fb = np.zeros((8, 4, 128), dtype=np.float32)
    for mc in range(4):
        Fb[2 * mc, mc, 0:64] = 1.0
        Fb[2 * mc + 1, mc, 64:128] = 1.0
    c["FBLK"] = np.ascontiguousarray(Fb.reshape(8, 512))
    p = np.arange(128)[:, None]
    j = np.arange(128)[None, :]
    P0 = np.where(p > j, 0.0, 1.0)
    P1 = np.where(p < j, 0.0, 1.0)
    Z = np.zeros((128, 384))
    c["MASKZ"] = np.ascontiguousarray(
        np.concatenate([P1, Z, P0], axis=1).astype(np.float32))
    half = np.arange(0, DH, 2, dtype=np.float32)
    inv_freq = (1.0 / 10000.0 ** (half / DH)).astype(np.float32)
    t = np.arange(SEQ, dtype=np.float32)
    freqs = (t[:, None] * inv_freq[None, :]).astype(np.float32)
    freqs = np.concatenate([freqs, freqs], axis=1)
    base = ((half + 0.4 * DH) / (1.4 * DH)).astype(np.float32)
    power = ((t - SEQ // 2) / float(WIN // 2)).astype(np.float32)
    scale = (base[None, :] ** power[:, None]).astype(np.float32)
    scale = np.concatenate([scale, scale], axis=1)
    cos = np.cos(freqs).astype(np.float32)
    sin = np.sin(freqs).astype(np.float32)
    t2 = lambda a: np.ascontiguousarray(np.concatenate([a, a], axis=0).astype(np.float32))
    c["CSQ"] = t2((cos * scale * 8.0).T)
    c["SSQ"] = t2((sin * scale * 8.0).T)
    c["CSK"] = t2((cos / scale).T)
    c["SSK"] = t2((sin / scale).T)
    return c


def pack_cols(inputs):
    cols, idx = [], {}

    def add(name, col):
        v = np.asarray(col, np.float32).reshape(-1)
        if v.size < 128:
            v = np.concatenate([v, np.zeros(128 - v.size, np.float32)])
        idx[name] = len(cols)
        cols.append(v)

    vec128 = lambda v: np.asarray(v, np.float32).reshape(8, 128).T
    bo = vec128(inputs["L0_attn_bo"])
    for dc in range(DC):
        add(f"bo{dc}", bo[:, dc])
    add("bq", inputs["L0_attn_bq"])
    add("bk", inputs["L0_attn_bk"])
    for l in range(2):
        qs = np.asarray(inputs["L0_ffn_q_scale"][l], np.float32)
        ks = np.asarray(inputs["L0_ffn_k_scale"][l], np.float32)
        add(f"qs{l}", np.tile(qs, 2))
        add(f"rqs{l}", np.tile(np.concatenate([qs[32:], qs[:32]]), 2))
        add(f"ks{l}", np.tile(ks, 2))
        add(f"rks{l}", np.tile(np.concatenate([ks[32:], ks[:32]]), 2))
        lg = vec128(inputs["L0_ffn_ln_g"][l])
        lb = vec128(inputs["L0_ffn_ln_b"][l])
        b2 = vec128(inputs["L0_ffn_b2"][l])
        for dc in range(DC):
            add(f"lg{l}_{dc}", lg[:, dc])
            add(f"lb{l}_{dc}", lb[:, dc])
            add(f"b2{l}_{dc}", b2[:, dc])
        b1 = np.asarray(inputs["L0_ffn_b1"][l], np.float32)
        b1a = np.zeros(HCH * 128, np.float32); b1a[:HID] = b1[:HID]
        b1g = np.zeros(HCH * 128, np.float32); b1g[:HID] = b1[HID:]
        b1a = b1a.reshape(HCH, 128).T
        b1g = b1g.reshape(HCH, 128).T
        for cc in range(HCH):
            add(f"b1a{l}_{cc}", b1a[:, cc])
            add(f"b1g{l}_{cc}", b1g[:, cc])
    return np.ascontiguousarray(np.stack(cols, axis=1)), idx


class MK:
    def __init__(self, cpak_width, cpak_idx, upto=6):
        self.upto = upto
        self.nc = bass.Bass("TRN2", target_bir_lowering=False)
        self.p = Prog()
        self.ci = cpak_idx
        self.cpw = cpak_width
        self._sbuf_stack = []
        nc = self.nc
        D = {}
        self.D = D

        def din(name, shape, dtype=F32):
            D[name] = nc.dram_tensor(name, list(shape), dtype, kind="ExternalInput").ap()

        din("x", (SEQ, DM), F32R)
        din("wq", (DM, DH)); din("wk", (DM, DH)); din("wv", (DM, DH)); din("wog", (DH, DM))
        din("wqkv", (2, DM, 3 * INNER))
        din("wol", (2, INNER, DM))
        din("w1", (2, DM, 2 * HID))
        din("w2", (2, HID, DM))
        din("CPAK", (128, cpak_width))
        for nm, shp in [("ONESC", (128, 1)),
                        ("ONESR", (1, 128)),
                        ("MASKZ", (128, 640)), ("CSQ", (128, SEQ)), ("SSQ", (128, SEQ)),
                        ("CSK", (128, SEQ)), ("SSK", (128, SEQ)), ("BVB", (128, DH))]:
            din(nm, shp)
        for nm, shp in [("IDENT", (128, 128)), ("R128", (128, 128)), ("EBLK", (128, 32)),
                        ("FBLK", (8, 512)),
                        ("ONESCR", (128, 1)), ("ONESRR", (1, 128))]:
            din(nm, shp, F32R)
        self.out = nc.dram_tensor("out", [SEQ, DM], F32, kind="ExternalOutput").ap()
        mkint = lambda nm, shp: nc.dram_tensor(nm, list(shp), BF16, kind="Internal").ap()
        self.scr_qkv = mkint("scr_qkv", (2, 8, 128, DC * 128))
        self.scr_v = mkint("scr_v", (2, DC, 128, INNER))
        self.scr_wo = mkint("scr_wo", (2, 4, 128, DM))
        self.scr_w1 = mkint("scr_w1", (2, 2 * HCH, 128, DC * 128))
        self.scr_w2 = mkint("scr_w2", (2, HCH, 128, DM))
        self.scr_x1 = nc.dram_tensor("scr_x1", [128, DC * SEQ], F32, kind="Internal").ap()
        self.scr_st = nc.dram_tensor("scr_st", [2, 2, SEQ], F32R, kind="Internal").ap()

    _sb_uid = 0

    def sbuf(self, name, shape, dtype, keys=None):
        MK._sb_uid += 1
        ctx = self.nc.sbuf_tensor(f"sb{MK._sb_uid}_{name}", list(shape), dtype)
        hdl = ctx.__enter__()
        keys = keys if keys is not None else [name]
        self._sbuf_stack.append((name, ctx, keys))
        for k in keys:
            self.p.seed(k, list(self.p.pending_seed))
        setattr(self, name, hdl)
        return hdl

    def free(self, *names):
        popped = []
        for _ in names:
            popped.append(self._sbuf_stack.pop())
        got = {nm for nm, _, _ in popped}
        assert got == set(names), f"free mismatch: {got} vs {set(names)}"
        for nm, ctx, keys in popped:
            self.p.pending_seed.extend(self.p.last_tokens(keys))
            ctx.__exit__(None, None, None)
            delattr(self, nm)

    def cp(self, name, n=128, p0=0):
        i = self.ci[name]
        return self.CP[p0:p0 + n, i:i + 1]

    def xt(self, dc, c0, n):
        return self.XT[:, dc * SEQ + c0: dc * SEQ + c0 + n]

    @staticmethod
    def r32(ap):
        return ap

    @staticmethod
    def rr(ap):
        return ap.bitcast(F32R)

    def mmg(self, ops):
        def fn(e):
            last = None
            for (out, lhsT, rhs, start, stop) in ops:
                last = e.matmul(out, lhsT, rhs, start=start, stop=stop)
            return last
        return fn

    def mmg_qkv(self, ops):
        def fn(e):
            last = None
            for (out, lhsT, rhs, start, stop) in ops:
                last = e.matmul(out, lhsT, rhs, start=start, stop=stop)
            return last
        return fn

    def mmg_v(self, ops):
        def fn(e):
            last = None
            for (out, lhsT, rhs, start, stop) in ops:
                last = e.matmul(out, lhsT, rhs, start=start, stop=stop)
            return last
        return fn

    def mmg_av(self, ops):
        def fn(e):
            last = None
            for (out, lhsT, rhs, start, stop) in ops:
                last = e.matmul(out, lhsT, rhs, start=start, stop=stop)
            return last
        return fn

    def mmg_wo(self, ops):
        def fn(e):
            last = None
            for (out, lhsT, rhs, start, stop) in ops:
                last = e.matmul(out, lhsT, rhs, start=start, stop=stop)
            return last
        return fn

    def mmg_w1a(self, ops):
        def fn(e):
            last = None
            for (out, lhsT, rhs, start, stop) in ops:
                last = e.matmul(out, lhsT, rhs, start=start, stop=stop)
            return last
        return fn

    def mmg_w1g(self, ops):
        def fn(e):
            last = None
            for (out, lhsT, rhs, start, stop) in ops:
                last = e.matmul(out, lhsT, rhs, start=start, stop=stop)
            return last
        return fn

    def mmg_w2(self, ops):
        def fn(e):
            last = None
            for (out, lhsT, rhs, start, stop) in ops:
                last = e.matmul(out, lhsT, rhs, start=start, stop=stop)
            return last
        return fn

    def build(self):
        em = self.p.emit
        self.sbuf("XT", [128, DC * SEQ], F32R,
                  keys=[f"XT{d}_{t}" for d in range(DC) for t in range(NT)])
        self.sbuf("CP", [128, self.cpw], F32, keys=["CP"])
        self.sbuf("MASKZ", [128, 640], BF16, keys=["MASKZ"])
        self.sbuf("R128", [128, 128], F32R, keys=["R128"])
        self.sbuf("IDENT", [128, 128], F32R, keys=["IDENT"])
        self.sbuf("EBLK", [128, 32], F32R, keys=["EBLK"])
        self.sbuf("FBLK", [8, 512], F32R, keys=["FBLK"])
        self.sbuf("ONESC", [128, 1], F32, keys=["ONESC"])
        self.sbuf("ONESCR", [128, 1], F32R, keys=["ONESCR"])
        self.sbuf("ONESRR", [1, 128], F32R, keys=["ONESRR"])
        self.sbuf("BVB", [128, DH], F32, keys=["BVB"])
        self.sbuf("XSV", [128, 4], F32, keys=["XSV"])
        self.PB = [self.nc.psum_tensor(f"pb{i}", [128, 512], F32).__enter__()
                   for i in range(8)]

        with self.nc.allow_low_precision(reason="f32r structural matmuls"):
            self.const_loads()
            self.p0_weight_casts()
            self.p0_load_x()
            if self.upto >= 1:
                self.p1()
            if self.upto >= 2:
                self.p2(0)
            if self.upto >= 3:
                self.p3(0)
            if self.upto >= 4:
                self.p2(1)
            if self.upto >= 5:
                self.p3(1)
            self.p4(final_ln=(self.upto >= 6))

        fw = [(("S", nm), self.p.cnt.get(("S", nm), 0)) for nm in ("so0", "so1")]
        self.p.items["sync"].append(
            (fw, lambda e: e.nop(), ("E", "sync"), "fence"))
        self.p.cnt[("E", "sync")] += 1

        sems = {}
        for e in ENGS:
            sems[("E", e)] = self.nc.semaphore(f"se_{e}").__enter__()
        for nm in sorted(self.p.dma_sems):
            sems[("S", nm)] = self.nc.semaphore(f"sd_{nm}").__enter__()
        p = self.p
        with self.nc.allow_low_precision(reason="f32r structural matmuls"):
            with self.nc.Block() as block:
                block.tensor(lambda e: p.replay_one("tensor", e, sems))
                block.vector(lambda e: p.replay_one("vector", e, sems))
                block.scalar(lambda e: p.replay_one("scalar", e, sems))
                block.gpsimd(lambda e: p.replay_one("gpsimd", e, sems))
                block.sync(lambda e: p.replay_one("sync", e, sems))
        return self.nc

    def const_loads(self):
        em, D = self.p.emit, self.D
        for nm, dst, key in [("CPAK", self.CP, "CP"),
                             ("MASKZ", self.MASKZ, "MASKZ"),
                             ("R128", self.R128, "R128"), ("IDENT", self.IDENT, "IDENT"),
                             ("EBLK", self.EBLK, "EBLK"), ("FBLK", self.FBLK, "FBLK"),
                             ("ONESC", self.ONESC, "ONESC"),
                             ("ONESCR", self.ONESCR, "ONESCR"), ("ONESRR", self.ONESRR, "ONESRR"),
                             ("BVB", self.BVB, "BVB")]:
            em("gpsimd", lambda e, dst=dst, src=D[nm]: e.dma_start(out=dst[:, :], in_=src),
               writes=[key], dma=True, dmasem="c0", tag=f"ld{nm}")

    def p0_weight_casts(self):
        em, D = self.p.emit, self.D

        def cast(dst, src, key, tag, sem):
            em("gpsimd", lambda e, dst=dst, src=src: e.dma_start(out=dst, in_=src),
               writes=[key], dma=True, dmasem=sem, tag=tag)

        self.sbuf("wqb", [128, DC * DH], BF16, keys=["wqb"])
        self.sbuf("wkb", [128, DC * DH], BF16, keys=["wkb"])
        self.sbuf("wvb", [128, DC * DH], BF16, keys=["wvb"])
        self.sbuf("wogb", [64, DM], BF16, keys=["wogb"])
        for nm, dst, key in [("wq", self.wqb, "wqb"), ("wk", self.wkb, "wkb"),
                             ("wv", self.wvb, "wvb")]:
            src = D[nm].rearrange("(dc p) h -> p dc h", p=128)
            dstv = dst.rearrange("p (dc h) -> p dc h", h=DH)
            cast(dstv, src, key, f"c_{nm}", "c0")
        cast(self.wogb[:, :], D["wog"], "wogb", "c_wog", "c0")
        for l in range(2):
            for mc in range(8):
                src = D["wqkv"][l, :, mc * 128:(mc + 1) * 128].rearrange("(dc p) m -> p dc m", p=128)
                dst = self.scr_qkv[l, mc].rearrange("p (dc m) -> p dc m", m=128)
                cast(dst, src, f"SQKV{l}_{mc}", f"cqkv{l}_{mc}", f"cq{l}")
            for dc in range(DC):
                src = D["wqkv"][l, dc * 128:(dc + 1) * 128, 2 * INNER:3 * INNER]
                cast(self.scr_v[l, dc], src, f"SV{l}_{dc}", f"cv{l}_{dc}", f"cq{l}")
            for kc in range(4):
                src = D["wol"][l, kc * 128:(kc + 1) * 128, :]
                cast(self.scr_wo[l, kc], src, f"SWO{l}_{kc}", f"cwo{l}_{kc}", f"cq{l}")
            for cc in range(2 * HCH):
                w = 128 if (cc % HCH) != HCH - 1 else HLAST
                c0 = (0 if cc < HCH else HID) + (cc % HCH) * 128
                src = D["w1"][l, :, c0:c0 + w].rearrange("(dc p) m -> p dc m", p=128)
                dst = self.scr_w1[l, cc].rearrange("p (dc m) -> p dc m", m=128)[:, :, 0:w]
                cast(dst, src, f"SW1_{l}_{cc}", f"cw1{l}_{cc}", f"ca{l}")
            for cpi in range(HCH):
                rows = 128 if cpi != HCH - 1 else HLAST
                src = D["w2"][l, cpi * 128:cpi * 128 + rows, :]
                cast(self.scr_w2[l, cpi][0:rows, :], src, f"SW2_{l}_{cpi}", f"cw2{l}_{cpi}", f"cb{l}")

        self.p.fin_group("c0", ["CP", "MASKZ", "R128", "IDENT", "EBLK", "FBLK",
                                 "ONESC", "ONESCR", "ONESRR", "BVB",
                                 "wqb", "wkb", "wvb", "wogb"])
        for l in range(2):
            self.p.fin_group(f"cq{l}", [f"SQKV{l}_{m}" for m in range(8)]
                             + [f"SV{l}_{d}" for d in range(DC)]
                             + [f"SWO{l}_{k}" for k in range(4)])
            self.p.fin_group(f"ca{l}", [f"SW1_{l}_{c}" for c in range(2 * HCH)])
            self.p.fin_group(f"cb{l}", [f"SW2_{l}_{c}" for c in range(HCH)])

    def p0_load_x(self):
        em, D, PB = self.p.emit, self.D, self.PB
        xs = self.sbuf("xstage", [128, 4 * DM], F32R, keys=["xs0", "xs1", "xs2", "xs3"])
        for tc in range(32):
            sl = tc % 4
            st = xs[:, sl * DM:(sl + 1) * DM]
            em("sync", lambda e, st=st, tc=tc: e.dma_start(out=st, in_=D["x"][tc * 128:(tc + 1) * 128, :]),
               writes=[f"xs{sl}"], dma=True, dmasem=f"xs{sl}", tag=f"ldx{tc}")
            for dc in range(DC):
                bi = 4 + (dc % 2)
                pb = PB[bi]
                em("tensor", lambda e, st=st, dc=dc, pb=pb:
                   e.transpose(pb[:, 0:128].bitcast(F32R), st[:, dc * 128:(dc + 1) * 128],
                               self.IDENT[:, :]),
                   reads=[f"xs{sl}", "IDENT"], writes=[f"PB{bi}"], tag=f"trx{tc}_{dc}")
                dst = self.xt(dc, tc * 128, 128)
                em("scalar", lambda e, dst=dst, pb=pb: e.copy(dst, pb[:, 0:128]),
                   reads=[f"PB{bi}"], writes=[f"XT{dc}_{tc // 4}"], tag=f"evx{tc}_{dc}")
        self.free("xstage")

    def ln_rows(self, ROWS, rk, tag, five=False, ROWSR=None, b1=6, b2=7):
        em, PB = self.p.emit, self.PB
        mu = ROWS[0:1, 0:512]; va = ROWS[0:1, 512:1024]
        if five:
            rs = ROWSR[0:1, 0:512]; nm = ROWSR[0:1, 512:1024]
            sc = ROWS[0:1, 1024:1536]
            sck = rk + "sc"
        else:
            rs = ROWS[0:1, 1024:1536]; nm = mu
            sc = rs
            sck = rk + "rs"
        em("vector", lambda e: e.tensor_scalar_mul(mu, PB[b1][0:1, 0:512], 1.0 / DM),
           reads=[f"PB{b1}"], writes=[rk + "mu"], tag=tag + "mu")
        em("vector", lambda e, sc=sc: e.tensor_mul(sc, mu, mu),
           reads=[rk + "mu"], writes=[sck], tag=tag + "mu2")
        em("vector", lambda e, sc=sc: e.scalar_tensor_tensor(
            out=va, in0=PB[b2][0:1, 0:512], scalar=1.0 / DM, in1=sc,
            op0=ALU.mult, op1=ALU.subtract),
           reads=[f"PB{b2}", sck], writes=[rk + "va"], tag=tag + "var")
        em("scalar", lambda e, sc=sc: e.activation(sc, va, AF.Ln, bias=np.float32(EPS)),
           reads=[rk + "va"], writes=[sck], tag=tag + "ln")
        rsout = rs if five else self.rr(rs)
        nmout = nm if five else self.rr(nm)
        em("scalar", lambda e, sc=sc, rsout=rsout: e.activation(rsout, sc, AF.Exp, scale=np.float32(-0.5)),
           reads=[sck], writes=[rk + "rs"], tag=tag + "exp")
        em("vector", lambda e, nmout=nmout: e.scalar_tensor_tensor(
            out=nmout, in0=mu, scalar=-1.0, in1=rs, op0=ALU.mult, op1=ALU.mult),
           reads=[rk + "mu", rk + "rs"],
           writes=[rk + "nm"] + ([] if five else [rk + "mu"]), tag=tag + "nmur")
        return rs, nm

    def ln_stats_tile(self, tt, shifted, SQ, sqkeys, tag, b1=6, b2=7):
        em, PB = self.p.emit, self.PB
        for dc in range(DC):
            sh = shifted and dc >= 4
            if not sh:
                rhs = self.xt(dc, tt * 512, 512); o0, owid = 0, 512
            elif tt > 0:
                rhs = self.xt(dc, tt * 512 - 1, 512); o0, owid = 0, 512
            else:
                rhs = self.xt(dc, 0, 511); o0, owid = 1, 511
            rkeys = [f"XT{dc}_{tt}"] + ([f"XT{dc}_{tt-1}"] if (sh and tt > 0) else [])
            if not sh:
                em("tensor", lambda e, rhs=rhs, dc=dc:
                   e.matmul(PB[b1][0:1, 0:512], self.ONESCR[:, :],
                            rhs, start=(dc == 0), stop=(dc == DC - 1)),
                   reads=rkeys + ["ONESCR"], writes=[f"PB{b1}"], tag=f"{tag}s1_{dc}")
            else:
                em("tensor", lambda e, rhs=rhs, o0=o0, owid=owid, dc=dc:
                   e.matmul(PB[b1][0:1, o0:o0 + owid], self.ONESC[:, :],
                            rhs.bitcast(F32), start=(dc == 0), stop=(dc == DC - 1)),
                   reads=rkeys + ["ONESC"], writes=[f"PB{b1}"], tag=f"{tag}s1_{dc}")
            sl = (dc % 2) if len(set(sqkeys)) > 1 else 0
            sq = SQ[:, sl * 512:(sl + 1) * 512]
            em("scalar", lambda e, sq=sq, rhs=rhs, owid=owid, o0=o0:
               e.activation(sq[:, o0:o0 + owid], rhs, AF.Square),
               reads=rkeys, writes=[sqkeys[sl]], tag=f"{tag}sq_{dc}")
            if o0 == 1:
                em("scalar", lambda e, sq=sq, dc=dc: e.activation(
                    sq[:, 0:1], self.xt(dc, 0, 1), AF.Square, scale=np.float32(0.0)),
                   reads=[], writes=[sqkeys[sl]], tag=f"{tag}sqz_{dc}")
            em("tensor", lambda e, sq=sq, dc=dc:
               e.matmul(PB[b2][0:1, 0:512], self.ONESCR[:, :],
                        sq[:, 0:512], start=(dc == 0), stop=(dc == DC - 1)),
               reads=[sqkeys[sl], "ONESCR"], writes=[f"PB{b2}"], tag=f"{tag}s2_{dc}")

    def ln_bcast(self, rs, nm, rk, tag):
        em, PB = self.p.emit, self.PB
        em("tensor", lambda e: e.matmul(PB[1][:, 0:512], self.ONESRR[:, :],
                                        self.rr(rs), start=True, stop=True),
           reads=[rk + "rs", "ONESRR"], writes=["PB1"], tag=tag + "bR")
        em("tensor", lambda e: e.matmul(PB[0][:, 0:512], self.ONESRR[:, :],
                                        self.rr(nm), start=True, stop=True),
           reads=[rk + "nm", "ONESRR"], writes=["PB0"], tag=tag + "bM")

    def p1(self):
        em, D, PB = self.p.emit, self.D, self.PB
        self.sbuf("XB", [128, 8 * 512], BF16, keys=[f"XB{d}" for d in range(DC)])
        XB = self.XB
        self.sbuf("QT", [64, SEQ], BF16, keys=[f"QT{t}" for t in range(NT)])
        QT = self.QT
        self.sbuf("KT", [64, SEQ], BF16, keys=[f"KT{t}" for t in range(NT)])
        KT = self.KT
        self.sbuf("VG", [128, 32 * 65], BF16, keys=[f"VG{tc}" for tc in range(32)] + ["VGones"])
        VG = self.VG
        self.sbuf("AEX", [128, 4 * 512], BF16, keys=["AEX0", "AEX1", "AEX2", "AEX3"])
        AEX = self.AEX
        self.sbuf("OSB", [64, 512], F32, keys=["OSB"])
        OSB = self.OSB
        self.sbuf("OBB", [64, 512], BF16, keys=["OBB"])
        OBB = self.OBB
        self.sbuf("RDEN", [1, 512], F32R, keys=["RDEN"])
        RDEN = self.RDEN
        self.sbuf("BQS", [64, 1], F32, keys=["BQS"])
        BQS = self.BQS
        self.sbuf("SQ1", [128, 2 * 512], F32R, keys=["SQ1a", "SQ1b"])
        SQ1 = self.SQ1
        self.sbuf("TMPA", [128, 512], F32, keys=["TMPA"])
        TMPA = self.TMPA
        self.sbuf("ROWS1", [1, 1536], F32,
                  keys=["L1mu", "L1va", "L1sc"])
        ROWSR1 = self.sbuf("ROWSR1", [1, 1024], F32R, keys=["L1rs", "L1nm"])
        ROWS1 = self.ROWS1

        em("scalar", lambda e: e.mul(BQS[:, :], self.cp("bq", 64), np.float32(0.125)),
           reads=["CP"], writes=["BQS"], tag="bqs")
        vgones = VG.rearrange("p (tc c) -> p tc c", c=65)[:, :, 64:65]
        em("vector", lambda e: e.memset(vgones, 1.0), writes=["VGones"], tag="vgones")

        for qt in range(NT):
            for dc in range(DC):
                em("vector", lambda e, dc=dc, qt=qt: e.tensor_copy(
                    XB[:, dc * 512:(dc + 1) * 512], self.xt(dc, qt * 512, 512)),
                   reads=[f"XT{dc}_{qt}"], writes=[f"XB{dc}"], tag=f"xb{qt}_{dc}")
            xbkeys = [f"XB{d}" for d in range(DC)]
            for (wname, wimg, bank, dstT, bias, scl, dkey) in [
                    ("q", self.wqb, 2, QT, BQS[:, :], 0.125, f"QT{qt}"),
                    ("k", self.wkb, 3, KT, self.cp("bk", 64), 1.0, f"KT{qt}")]:
                ops = []
                for dc in range(DC):
                    ops.append((PB[bank][0:64, 0:512],
                                wimg[:, dc * DH:(dc + 1) * DH],
                                XB[:, dc * 512:(dc + 1) * 512],
                                dc == 0, dc == DC - 1))
                em("tensor", self.mmg(ops), reads=xbkeys + ["wqb", "wkb"],
                   writes=[f"PB{bank}"], tag=f"{wname}mm{qt}")
                dst = dstT[:, qt * 512:(qt + 1) * 512]
                em("scalar", lambda e, dst=dst, bank=bank, bias=bias, scl=scl:
                   e.activation(dst, PB[bank][0:64, 0:512], AF.Identity, bias=bias, scale=np.float32(scl)),
                   reads=[f"PB{bank}", "BQS", "CP"], writes=[dkey], tag=f"{wname}ev{qt}")
            for t4 in range(4):
                bank = (t4 % 2)
                ops = []
                for dc in range(DC):
                    ops.append((PB[bank][:, 0:DH],
                                XB[:, dc * 512 + t4 * 128: dc * 512 + (t4 + 1) * 128],
                                self.wvb[:, dc * DH:(dc + 1) * DH],
                                dc == 0, dc == DC - 1))
                em("tensor", self.mmg(ops), reads=xbkeys + ["wvb"],
                   writes=[f"PB{bank}"], tag=f"vmm{qt}_{t4}")
                tc = qt * 4 + t4
                dst = VG[:, tc * 65: tc * 65 + 64]
                em("vector", lambda e, dst=dst, bank=bank: e.tensor_tensor(
                    dst, PB[bank][:, 0:DH], self.BVB[:, :], ALU.add),
                   reads=[f"PB{bank}", "BVB"], writes=[f"VG{tc}"], tag=f"vev{qt}_{t4}")
            nkc = 4 * qt + 4
            pendav = []

            def gflush(keep=0):
                while len(pendav) > keep:
                    fn = pendav.pop(0); fn()

            for kc in range(nkc):
                par = kc % 4
                sb = [4, 5, 2][kc % 3]
                em("tensor", lambda e, kc=kc, qt=qt, sb=sb: e.matmul(
                    PB[sb][:, 0:512], KT[:, kc * 128:(kc + 1) * 128],
                    QT[:, qt * 512:(qt + 1) * 512], start=True, stop=True),
                   reads=[f"KT{kc // 4}", f"QT{qt}"], writes=[f"PB{sb}"], tag=f"gS{qt}_{kc}")
                em("scalar", lambda e, par=par, sb=sb: e.activation(
                    AEX[:, par * 512:(par + 1) * 512], PB[sb][:, 0:512], AF.Exp),
                   reads=[f"PB{sb}"], writes=[f"AEX{par}"], tag=f"gE{qt}_{kc}")
                r = kc - 4 * qt
                if r >= 0:
                    a = AEX[:, par * 512:(par + 1) * 512]
                    em("vector", lambda e, a=a, r=r: e.tensor_tensor(
                        a[:, 0:(r + 1) * 128], a[:, 0:(r + 1) * 128],
                        self.MASKZ[:, 512 - r * 128:640], ALU.mult),
                       reads=["MASKZ"], writes=[f"AEX{par}"], tag=f"gM{qt}_{kc}")
                gflush(keep=2)
                ob = 6 + (qt % 2)

                def mkgav(kc=kc, par=par, nkc=nkc, ob=ob):
                    def doit():
                        em("tensor", lambda e, kc=kc, par=par, nkc=nkc, ob=ob: e.matmul(
                            PB[ob][0:65, 0:512], VG[:, kc * 65:(kc + 1) * 65],
                            AEX[:, par * 512:(par + 1) * 512],
                            start=(kc == 0), stop=(kc == nkc - 1)),
                           reads=[f"AEX{par}", f"VG{kc}", "VGones"], writes=[f"PB{ob}"],
                           tag=f"gAV{qt}_{kc}")
                    return doit

                pendav.append(mkgav())
            gflush()
            ob = 6 + (qt % 2)
            em("scalar", lambda e, ob=ob: e.activation(
                OSB[0:1, 0:512], PB[ob][64:65, 0:512], AF.Ln),
               reads=[f"PB{ob}"], writes=["OSB"], tag=f"gRa{qt}")
            em("scalar", lambda e: e.activation(
                RDEN[:, :], OSB[0:1, 0:512], AF.Exp, scale=np.float32(-1.0)),
               reads=["OSB"], writes=["RDEN"], tag=f"gR{qt}")
            em("tensor", lambda e: e.matmul(PB[3][0:64, 0:512],
                                            self.ONESRR[:, 0:64],
                                            RDEN[:, :], start=True, stop=True),
               reads=["RDEN", "ONESRR"], writes=["PB3"], tag=f"gB{qt}")
            em("scalar", lambda e, ob=ob: e.copy(OSB[:, :], PB[ob][0:64, 0:512]),
               reads=[f"PB{ob}"], writes=["OSB"], tag=f"gO{qt}")
            em("vector", lambda e: e.tensor_tensor(OBB[:, :], OSB[:, :],
                                                   PB[3][0:64, 0:512], ALU.mult),
               reads=["OSB", "PB3"], writes=["OBB"], tag=f"gN{qt}")
            for mc in range(DC):
                zb = mc % 2
                em("tensor", lambda e, mc=mc, zb=zb: e.matmul(
                    PB[zb][:, 0:512], self.wogb[:, mc * 128:(mc + 1) * 128],
                    OBB[:, :], start=True, stop=True),
                   reads=["OBB", "wogb"], writes=[f"PB{zb}"], tag=f"gW{qt}_{mc}")
                dst = self.xt(mc, qt * 512, 512)
                em("vector", lambda e, dst=dst, zb=zb, mc=mc: e.scalar_tensor_tensor(
                    out=dst, in0=PB[zb][:, 0:512], scalar=self.cp(f"bo{mc}"), in1=dst,
                    op0=ALU.add, op1=ALU.add),
                   reads=[f"PB{zb}", "CP", f"XT{mc}_{qt}"], writes=[f"XT{mc}_{qt}"],
                   tag=f"gres{qt}_{mc}")
        def l1fin(tt):
            b1, b2 = (6, 7) if tt % 2 == 0 else (2, 3)
            rs, nm = self.ln_rows(ROWS1, "L1", f"L1r{tt}_", five=True, ROWSR=ROWSR1,
                                  b1=b1, b2=b2)
            self.ln_bcast(rs, nm, "L1", f"L1b{tt}_")
            for dc in range(DC):
                src = self.xt(dc, tt * 512, 512)
                em("vector", lambda e, src=src: e.tensor_tensor(
                    TMPA[:, :], src, PB[1][:, 0:512], ALU.mult),
                   reads=[f"XT{dc}_{tt}", "PB1"], writes=["TMPA"], tag=f"L1a{tt}_{dc}")
                em("vector", lambda e, src=src: e.tensor_tensor(
                    src, TMPA[:, :], PB[0][:, 0:512], ALU.add),
                   reads=["TMPA", "PB0"], writes=[f"XT{dc}_{tt}"], tag=f"L1b{tt}_{dc}")

        for tt in range(NT):
            b1, b2 = (6, 7) if tt % 2 == 0 else (2, 3)
            self.ln_stats_tile(tt, False, SQ1, ["SQ1a", "SQ1b"], f"L1s{tt}_", b1=b1, b2=b2)
            if tt > 0:
                l1fin(tt - 1)
        l1fin(NT - 1)
        for dc in range(DC):
            em("sync", lambda e, dc=dc: e.dma_start(
                out=self.scr_x1[:, dc * SEQ:(dc + 1) * SEQ],
                in_=self.XT[:, dc * SEQ:(dc + 1) * SEQ].bitcast(F32)),
               reads=[f"XT{dc}_{t}" for t in range(NT)], writes=[f"X1S{dc}"],
               dma=True, dmasem=f"x1s{dc}", tag=f"x1sp{dc}")
        self.free("XB", "QT", "KT", "VG", "AEX", "OSB", "OBB", "RDEN", "BQS",
                  "SQ1", "TMPA", "ROWS1", "ROWSR1")

    def p2(self, l):
        em, D, PB = self.p.emit, self.D, self.PB
        self.sbuf("HT", [128, 8 * 512], BF16, keys=[f"HT{d}" for d in range(DC)])
        HT = self.HT
        self.sbuf("QKVB", [128, 2 * 1024], BF16, keys=["QKVB0", "QKVB1"])
        QKVB = self.QKVB
        self.sbuf("VWB", [128, 2 * 512], BF16, keys=["VWB0", "VWB1"])
        VWB = self.VWB
        self.sbuf("WOSB", [128, 2 * 1024], BF16, keys=["WOSB0", "WOSB1"])
        WOSB = self.WOSB
        self.sbuf("TAB0", [128, 512], F32, keys=["TAB0"])
        TAB0 = self.TAB0
        self.sbuf("TAB1", [128, 512], F32, keys=["TAB1"])
        TAB1 = self.TAB1
        self.sbuf("TAB2", [128, 512], F32, keys=["TAB2"])
        TAB2 = self.TAB2
        self.sbuf("RAW", [128, 4 * 512], F32R, keys=["RAW0", "RAW1", "RAW2", "RAW3"])
        RAW = self.RAW
        self.sbuf("QN", [128, 512], F32R, keys=["QN"])
        QN = self.QN
        self.sbuf("T1", [128, 512], F32, keys=["T1"])
        T1 = self.T1
        self.sbuf("T2", [128, 512], F32R, keys=["T2"])
        T2 = self.T2
        self.sbuf("RNX", [8, 512], F32R, keys=["RNX"])
        RNQ = self.RNX
        RNK = self.RNX
        self.sbuf("RNKT", [128, 64], F32, keys=["RNKT0", "RNKT1"])
        RNKT = self.RNKT
        self.sbuf("Q2B", [128, 4 * 512], BF16, keys=["Q2B0", "Q2B1", "Q2B2", "Q2B3"])
        Q2B = self.Q2B
        self.sbuf("K2B", [128, 2 * 2048], BF16,
                  keys=[f"K2B{sl}_{c}" for sl in "ab" for c in range(4)])
        K2B = self.K2B
        self.sbuf("VL", [128, 2 * 4 * 520], BF16,
                  keys=["VLa", "VLb", "VLones"])
        VL = self.VL
        self.sbuf("OT", [128, 4 * 512], BF16, keys=["OT0", "OT1", "OT2", "OT3"])
        OT = self.OT
        self.sbuf("AEX", [128, 4 * 512], BF16, keys=["AEX0", "AEX1", "AEX2", "AEX3"])
        AEX = self.AEX

        vlones = VL.rearrange("p (s c) -> p s c", c=65)[:, :, 64:65]
        em("vector", lambda e: e.memset(vlones, 1.0), writes=["VLones"], tag=f"vlone{l}")
        for sl in range(4):
            em("vector", lambda e, sl=sl: e.memset(AEX[:, sl * 512:(sl + 1) * 512], 0.0),
               writes=[f"AEX{sl}"], tag=f"aexz{l}_{sl}")

        def emit_ht(w):
            wk = f"w{l}_{w}"
            for dc in range(DC):
                dst = HT[:, dc * 512:(dc + 1) * 512]
                if dc < 4:
                    em("vector", lambda e, dst=dst, dc=dc, w=w: e.tensor_copy(
                        dst, self.xt(dc, w * 512, 512)),
                       reads=[f"XT{dc}_{w}"], writes=[f"HT{dc}"], tag=f"{wk}h{dc}")
                elif w == 0:
                    em("vector", lambda e, dst=dst: e.memset(dst[:, 0:1], 0.0),
                       reads=[], writes=[f"HT{dc}"], tag=f"{wk}h{dc}z")
                    em("vector", lambda e, dst=dst, dc=dc: e.tensor_copy(
                        dst[:, 1:512], self.xt(dc, 0, 511)),
                       reads=[f"XT{dc}_0"], writes=[f"HT{dc}"], tag=f"{wk}h{dc}")
                else:
                    em("vector", lambda e, dst=dst, dc=dc: e.tensor_copy(
                        dst[:, 0:1], self.XSV[:, dc - 4:dc - 3]),
                       reads=["XSV"], writes=[f"HT{dc}"], tag=f"{wk}h{dc}s")
                    em("vector", lambda e, dst=dst, dc=dc, w=w: e.tensor_copy(
                        dst[:, 1:512], self.xt(dc, w * 512, 511)),
                       reads=[f"XT{dc}_{w}"], writes=[f"HT{dc}"], tag=f"{wk}h{dc}")

        def emit_tabs(w):
            wk = f"w{l}_{w}"
            for tb, nm, tbk, sem in [(TAB0, "CSQ", "TAB0", "tb0"),
                                     (TAB1, "SSQ", "TAB1", "tb1"),
                                     (TAB2, "CSK", "TAB2", "tb2")]:
                em("gpsimd", lambda e, tb=tb, nm=nm, w=w: e.dma_start(
                    out=tb[:, :], in_=D[nm][:, w * 512:(w + 1) * 512]),
                   writes=[tbk], dma=True, dmasem=sem, tag=f"{wk}tb{nm}")

        emit_ht(0)
        emit_tabs(0)
        for w in range(NW):
            ks = w % 2
            pks = 1 - ks
            wk = f"w{l}_{w}"
            htk = [f"HT{d}" for d in range(DC)]
            if w < NW - 1:
                for dc in range(4, 8):
                    em("vector", lambda e, dc=dc, w=w: e.tensor_copy(
                        self.XSV[:, dc - 4:dc - 3], self.xt(dc, w * 512 + 511, 1)),
                       reads=[f"XT{dc}_{w}"], writes=["XSV"], tag=f"{wk}xs{dc}")

            pend_n = [None]

            def flushn():
                if pend_n[0] is not None:
                    fn = pend_n[0]
                    pend_n[0] = None
                    fn()

            def proj(mc):
                qsl = mc % 2
                em("sync", lambda e, l=l, mc=mc, qsl=qsl: e.dma_start(
                    out=QKVB[:, qsl * 1024:(qsl + 1) * 1024], in_=self.scr_qkv[l, mc]),
                   reads=[f"SQKV{l}_{mc}"], writes=[f"QKVB{qsl}"], dma=True,
                   dmasem=f"qkvb{qsl}", tag=f"{wk}qkd{mc}")
                bank = [0, 1, 4, 5][mc % 4]
                for dc in range(DC):
                    ops = [(PB[bank][:, 0:512],
                            QKVB[:, qsl * 1024 + dc * 128: qsl * 1024 + (dc + 1) * 128],
                            HT[:, dc * 512:(dc + 1) * 512],
                            dc == 0, dc == DC - 1)]
                    em("tensor", self.mmg_qkv(ops), reads=[f"HT{dc}", f"QKVB{qsl}"],
                       writes=[f"PB{bank}"], tag=f"{wk}qkm{mc}_{dc}")
                flushn()
                ri = mc % 4
                if mc < 4:
                    sqa, sqk = (QN, "QN") if mc % 2 == 0 else (T2, "T2")
                else:
                    sqa, sqk = T2, "T2"
                em("scalar", lambda e, ri=ri, bank=bank: e.copy(
                    RAW[:, ri * 512:(ri + 1) * 512], PB[bank][:, 0:512]),
                   reads=[f"PB{bank}"], writes=[f"RAW{ri}"], tag=f"{wk}qke{mc}")
                em("scalar", lambda e, bank=bank, sqa=sqa: e.activation(
                    sqa[:, 0:512], PB[bank][:, 0:512], AF.Square),
                   reads=[f"PB{bank}"], writes=[sqk], tag=f"{wk}qks{mc}")
                nb = 2 if mc < 4 else 3

                def doit(mc=mc, nb=nb, sqa=sqa, sqk=sqk):
                    em("tensor", lambda e, mc=mc, nb=nb, sqa=sqa: e.matmul(
                        PB[nb][0:8, 0:512],
                        self.EBLK[:, (mc % 4) * 8:(mc % 4) * 8 + 8],
                        sqa[:, 0:512],
                        start=(mc % 4 == 0), stop=(mc % 4 == 3)),
                       reads=[sqk, "EBLK"],
                       writes=[f"PB{nb}"], tag=f"{wk}qkn{mc}")

                pend_n[0] = doit

            for mc in range(4):
                proj(mc)
            flushn()
            em("vector", lambda e: e.tensor_scalar_max(RNQ[:, :], PB[2][0:8, 0:512], 1e-24),
               reads=["PB2"], writes=["RNX"], tag=f"{wk}rnq0")
            em("scalar", lambda e: e.activation(RNQ[:, :], RNQ[:, :], AF.Ln),
               reads=["RNX"], writes=["RNX"], tag=f"{wk}rnq1")
            em("scalar", lambda e: e.activation(RNQ[:, :], RNQ[:, :], AF.Exp, scale=np.float32(-0.5)),
               reads=["RNX"], writes=["RNX"], tag=f"{wk}rnq2")
            for dc in range(DC):
                vs = dc % 2
                em("sync", lambda e, l=l, dc=dc, vs=vs: e.dma_start(
                    out=VWB[:, vs * 512:(vs + 1) * 512], in_=self.scr_v[l, dc]),
                   reads=[f"SV{l}_{dc}"], writes=[f"VWB{vs}"], dma=True,
                   dmasem=f"vwb{vs}", tag=f"{wk}vd{dc}")
                vbanks = [0, 1, 6, 7]
                ops = []
                for t4 in range(4):
                    ops.append((PB[vbanks[t4]][:, 0:512],
                                HT[:, dc * 512 + t4 * 128: dc * 512 + (t4 + 1) * 128],
                                VWB[:, vs * 512:(vs + 1) * 512],
                                dc == 0, dc == DC - 1))
                em("tensor", self.mmg_v(ops), reads=[f"HT{dc}", f"VWB{vs}"],
                   writes=["PB0", "PB1", "PB6", "PB7"], tag=f"{wk}vm{dc}")
            for t4 in range(4):
                vb = [0, 1, 6, 7][t4]
                dst = VL[:, ks * 2080 + t4 * 520: ks * 2080 + (t4 + 1) * 520]
                dstv = dst.rearrange("p (h c) -> p h c", c=65)[:, :, 0:64]
                em("vector", lambda e, dstv=dstv, vb=vb: e.tensor_copy(
                    dstv, PB[vb][:, 0:512].rearrange("p (h c) -> p h c", c=64)),
                   reads=[f"PB{vb}"], writes=[f"VL{'ab'[ks]}"], tag=f"{wk}ve{t4}")
            for qc in range(4):
                rb = [4, 0][qc % 2]
                ro = [5, 1][qc % 2]
                em("tensor", lambda e, qc=qc, rb=rb: e.matmul(
                    PB[rb][:, 0:512], self.FBLK[:, qc * 128:(qc + 1) * 128],
                    RNQ[:, :], start=True, stop=True),
                   reads=["RNX", "FBLK"], writes=[f"PB{rb}"], tag=f"{wk}qb{qc}")
                em("vector", lambda e, qc=qc, rb=rb: e.tensor_tensor(
                    QN[:, :], RAW[:, qc * 512:(qc + 1) * 512],
                    PB[rb][:, 0:512], ALU.mult),
                   reads=[f"RAW{qc}", f"PB{rb}"], writes=["QN"], tag=f"{wk}qn{qc}")
                proj(4 + qc)
                flushn()
                em("tensor", lambda e, ro=ro: e.matmul(
                    PB[ro][:, 0:512], self.R128[:, :],
                    QN[:, :], start=True, stop=True),
                   reads=["QN", "R128"], writes=[f"PB{ro}"], tag=f"{wk}qr{qc}")
                em("vector", lambda e, l=l: e.scalar_tensor_tensor(
                    out=T1[:, :], in0=QN[:, :], scalar=self.cp(f"qs{l}"),
                    in1=TAB0[:, :], op0=ALU.mult, op1=ALU.mult),
                   reads=["QN", "CP", "TAB0"], writes=["T1"], tag=f"{wk}q1{qc}")
                em("vector", lambda e, l=l, ro=ro: e.scalar_tensor_tensor(
                    out=T2[:, :], in0=PB[ro][:, 0:512], scalar=self.cp(f"rqs{l}"),
                    in1=TAB1[:, :], op0=ALU.mult, op1=ALU.mult),
                   reads=[f"PB{ro}", "CP", "TAB1"], writes=["T2"], tag=f"{wk}q2{qc}")
                em("vector", lambda e, qc=qc: e.tensor_tensor(
                    Q2B[:, qc * 512:(qc + 1) * 512], T1[:, :], T2[:, :], ALU.add),
                   reads=["T1", "T2"], writes=[f"Q2B{qc}"], tag=f"{wk}q3{qc}")
            em("gpsimd", lambda e, w=w: e.dma_start(
                out=TAB1[:, :], in_=D["SSK"][:, w * 512:(w + 1) * 512]),
               writes=["TAB1"], dma=True, dmasem="tb1", tag=f"{wk}tbSSK")
            em("vector", lambda e: e.tensor_scalar_max(RNK[:, :], PB[3][0:8, 0:512], 1e-24),
               reads=["PB3"], writes=["RNX"], tag=f"{wk}rnk0")
            em("scalar", lambda e: e.activation(RNK[:, :], RNK[:, :], AF.Ln),
               reads=["RNX"], writes=["RNX"], tag=f"{wk}rnk1")
            em("scalar", lambda e: e.activation(RNK[:, :], RNK[:, :], AF.Exp, scale=np.float32(-0.5)),
               reads=["RNX"], writes=["RNX"], tag=f"{wk}rnk2")
            for sblk in range(4):
                em("tensor", lambda e, sblk=sblk: e.transpose(
                    PB[3][:, 0:8], RNK[:, sblk * 128:(sblk + 1) * 128].bitcast(F32),
                    self.IDENT[0:8, 0:8].bitcast(F32)),
                   reads=["RNX", "IDENT"], writes=["PB3"], tag=f"{wk}rnt{sblk}")
                em("vector", lambda e, sblk=sblk, ks=ks: e.tensor_copy(
                    RNKT[:, ks * 32 + sblk * 8: ks * 32 + (sblk + 1) * 8], PB[3][:, 0:8]),
                   reads=["PB3"], writes=[f"RNKT{ks}"], tag=f"{wk}rne{sblk}")
            for qc in range(4):
                ro = [2, 3][qc % 2]
                em("vector", lambda e, qc=qc, l=l: e.scalar_tensor_tensor(
                    out=T1[:, :], in0=RAW[:, qc * 512:(qc + 1) * 512],
                    scalar=self.cp(f"ks{l}"), in1=TAB2[:, :], op0=ALU.mult, op1=ALU.mult),
                   reads=[f"RAW{qc}", "CP", "TAB2"], writes=["T1"], tag=f"{wk}k1{qc}")
                em("tensor", lambda e, qc=qc, ro=ro: e.matmul(
                    PB[ro][:, 0:512], self.R128[:, :],
                    RAW[:, qc * 512:(qc + 1) * 512], start=True, stop=True),
                   reads=[f"RAW{qc}", "R128"], writes=[f"PB{ro}"], tag=f"{wk}kr{qc}")
                em("vector", lambda e, l=l, ro=ro: e.scalar_tensor_tensor(
                    out=T2[:, :], in0=PB[ro][:, 0:512], scalar=self.cp(f"rks{l}"),
                    in1=TAB1[:, :], op0=ALU.mult, op1=ALU.mult),
                   reads=[f"PB{ro}", "CP", "TAB1"], writes=["T2"], tag=f"{wk}k2{qc}")
                em("vector", lambda e, qc=qc, ks=ks: e.tensor_tensor(
                    K2B[:, ks * 2048 + qc * 512: ks * 2048 + (qc + 1) * 512],
                    T1[:, :], T2[:, :], ALU.add),
                   reads=["T1", "T2"], writes=[f"K2B{'ab'[ks]}_{qc}"], tag=f"{wk}k3{qc}")

            pend = {"av": [], "norm": None}

            def flush_av(keep=0):
                while len(pend["av"]) > keep:
                    fn = pend["av"].pop(0)
                    fn()

            def flush_norm():
                if pend["norm"] is not None:
                    fn = pend["norm"]
                    pend["norm"] = None
                    fn()

            for h in range(HEADS):
                ch = h // 2
                pb0 = 64 * (h % 2)
                kclist = list(range(8)) if w > 0 else list(range(4, 8))
                avwr = 0
                for i, kc in enumerate(kclist):
                    par = i % 4
                    sb = [4, 5, 2][i % 3]
                    qlo = max(0, (kc - 4) * 128)
                    qhi = min(512, kc * 128 + 128)
                    wd = qhi - qlo
                    if kc < 4:
                        ksl = pks
                        kk = f"K2B{'ab'[pks]}_{ch}"
                        vk = f"VL{'ab'[pks]}"
                        rk = f"RNKT{pks}"
                        rsl = pks
                    else:
                        ksl = ks
                        kk = f"K2B{'ab'[ks]}_{ch}"
                        vk = f"VL{'ab'[ks]}"
                        rk = f"RNKT{ks}"
                        rsl = ks
                    lk = K2B[pb0:pb0 + 64,
                             ksl * 2048 + ch * 512 + (kc % 4) * 128:
                             ksl * 2048 + ch * 512 + (kc % 4) * 128 + 128]
                    rq = Q2B[pb0:pb0 + 64, ch * 512 + qlo: ch * 512 + qhi]
                    em("tensor", lambda e, lk=lk, rq=rq, sb=sb, qlo=qlo, wd=wd: e.matmul(
                        PB[sb][:, qlo:qlo + wd], lk, rq, start=True, stop=True),
                       reads=[kk, f"Q2B{ch}"], writes=[f"PB{sb}"], tag=f"{wk}S{h}_{kc}")
                    if i == 3 or (i == 1 and len(kclist) == 4):
                        flush_norm()
                    scl = RNKT[:, rsl * 32 + (kc % 4) * 8 + h: rsl * 32 + (kc % 4) * 8 + h + 1]
                    aex = AEX[:, par * 512:(par + 1) * 512]
                    em("scalar", lambda e, aex=aex, sb=sb, scl=scl, qlo=qlo, wd=wd: e.activation(
                        aex[:, qlo:qlo + wd], PB[sb][:, qlo:qlo + wd], AF.Exp, scale=scl),
                       reads=[f"PB{sb}", rk], writes=[f"AEX{par}"], tag=f"{wk}E{h}_{kc}")
                    if kc >= 4:
                        mq = (kc - 4) * 128
                        em("vector", lambda e, aex=aex, mq=mq: e.tensor_tensor(
                            aex[:, mq:mq + 128], aex[:, mq:mq + 128],
                            self.MASKZ[:, 512:640], ALU.mult),
                           reads=["MASKZ"], writes=[f"AEX{par}"], tag=f"{wk}M{h}_{kc}")
                    else:
                        mq = kc * 128
                        em("vector", lambda e, aex=aex, mq=mq: e.tensor_tensor(
                            aex[:, mq:mq + 128], aex[:, mq:mq + 128],
                            self.MASKZ[:, 0:128], ALU.mult),
                           reads=["MASKZ"], writes=[f"AEX{par}"], tag=f"{wk}M{h}_{kc}")
                    flush_av(keep=2)
                    vt = VL[:, ksl * 2080 + (kc % 4) * 520 + h * 65:
                            ksl * 2080 + (kc % 4) * 520 + (h + 1) * 65]
                    ob = 6 + (h % 2)
                    parts = []
                    if qlo < avwr:
                        parts.append((qlo, min(qhi, avwr)))
                    if qhi > avwr:
                        parts.append((max(qlo, avwr), qhi))
                    avops = []
                    for (p0, p1) in parts:
                        avops.append((PB[ob][0:65, p0:p1], vt,
                                      aex[:, p0:p1],
                                      (i == 0 and p0 == qlo),
                                      (i == len(kclist) - 1 and p1 == qhi)))
                    avwr = max(avwr, qhi)

                    def mkav(avops=avops, par=par, vk=vk, h=h, kc=kc):
                        def doit():
                            em("tensor", self.mmg_av(avops),
                               reads=[f"AEX{par}", vk, "VLones"], writes=[f"PB{6 + (h % 2)}"],
                               tag=f"{wk}AV{h}_{kc}")
                        return doit

                    pend["av"].append(mkav())
                flush_av()
                ob0 = 6 + (h % 2)
                em("scalar", lambda e, ob=ob0: e.activation(
                    RNK[0:1, 0:512], PB[ob][64:65, 0:512], AF.Ln),
                   reads=[f"PB{ob0}"], writes=["RNX"], tag=f"{wk}Ra{h}")
                em("scalar", lambda e: e.activation(
                    T2[0:1, 0:512], RNK[0:1, 0:512], AF.Exp, scale=np.float32(-1.0)),
                   reads=["RNX"], writes=["T2"], tag=f"{wk}R{h}")

                def mknorm(h=h, ch=ch, pb0=pb0):
                    def doit():
                        ob = 6 + (h % 2)
                        em("tensor", lambda e: e.matmul(
                            PB[3][0:64, 0:512], self.ONESRR[:, 0:64],
                            T2[0:1, 0:512], start=True, stop=True),
                           reads=["T2", "ONESRR"], writes=["PB3"], tag=f"{wk}B{h}")
                        em("vector", lambda e, ob=ob: e.tensor_copy(T1[0:64, :], PB[ob][0:64, 0:512]),
                           reads=[f"PB{ob}"], writes=["T1"], tag=f"{wk}O{h}")
                        dst = OT[pb0:pb0 + 64, ch * 512:(ch + 1) * 512]
                        em("vector", lambda e, dst=dst: e.tensor_tensor(
                            dst, T1[0:64, :], PB[3][0:64, 0:512], ALU.mult),
                           reads=["T1", "PB3"], writes=[f"OT{ch}"], tag=f"{wk}N{h}")
                    return doit

                pend["norm"] = mknorm()
            flush_norm()
            if w < NW - 1:
                emit_ht(w + 1)
                emit_tabs(w + 1)
            otk = [f"OT{c}" for c in range(4)]
            for kc4 in range(4):
                wsl = kc4 % 2
                em("gpsimd", lambda e, l=l, kc4=kc4, wsl=wsl: e.dma_start(
                    out=WOSB[:, wsl * 1024:(wsl + 1) * 1024], in_=self.scr_wo[l, kc4]),
                   reads=[f"SWO{l}_{kc4}"], writes=[f"WOSB{wsl}"], dma=True,
                   dmasem=f"wosb{wsl}", tag=f"{wk}wod{kc4}")
                ops = []
                for mc in range(DC):
                    ops.append((PB[mc][:, 0:512],
                                WOSB[:, wsl * 1024 + mc * 128: wsl * 1024 + (mc + 1) * 128],
                                OT[:, kc4 * 512:(kc4 + 1) * 512],
                                kc4 == 0, kc4 == 3))
                em("tensor", self.mmg_wo(ops), reads=otk + [f"WOSB{wsl}"],
                   writes=[f"PB{m}" for m in range(8)], tag=f"{wk}wom{kc4}")
            for mc in range(DC):
                dst = self.xt(mc, w * 512, 512)
                em("vector", lambda e, dst=dst, mc=mc: e.tensor_tensor(
                    dst, PB[mc][:, 0:512], dst, ALU.add),
                   reads=[f"PB{mc}", f"XT{mc}_{w}"], writes=[f"XT{mc}_{w}"],
                   tag=f"{wk}res{mc}")
        self.free("HT", "QKVB", "VWB", "WOSB", "TAB0", "TAB1", "TAB2", "RAW", "QN",
                  "T1", "T2", "RNX", "RNKT", "Q2B", "K2B", "VL",
                  "OT", "AEX")

    def p3(self, l):
        em, PB = self.p.emit, self.PB
        self.sbuf("U", [128, HCH * 512], BF16, keys=[f"U{c}" for c in range(HCH)])
        U = self.U
        self.sbuf("H2", [128, 8 * 512], BF16, keys=[f"H2{d}" for d in range(DC)])
        H2 = self.H2
        self.sbuf("W1A", [128, 2 * 1024], BF16, keys=["W1A0", "W1A1"])
        W1A = self.W1A
        self.sbuf("W1G", [128, 2 * 1024], BF16, keys=["W1G0", "W1G1"])
        W1G = self.W1G
        self.sbuf("W2B", [128, 3 * 1024], BF16, keys=["W2B0", "W2B1", "W2B2"])
        W2B = self.W2B
        self.sbuf("GEL", [128, 512], F32, keys=["GEL0"])
        GEL = self.GEL
        self.sbuf("TMP1", [128, 512], F32, keys=["TMP1a"])
        TMP1 = self.TMP1
        self.sbuf("SQ3", [128, 512], F32R, keys=["SQ3a"])
        SQ3 = self.SQ3
        self.sbuf("ROWS3", [1, 1536], F32, keys=["L3mu", "L3va", "L3rs", "L3nm"])
        ROWS3 = self.ROWS3
        self.sbuf("X1B", [128, 2 * 512], F32, keys=["X1B0", "X1B1"])
        X1B = self.X1B
        self.sbuf("RST", [1, 2 * 512], F32R, keys=["RST0", "RST1"])
        RST = self.RST
        self.sbuf("NMU", [1, 2 * 512], F32R, keys=["NMU0", "NMU1"])
        NMU = self.NMU

        for tt in range(NT):
            tk = f"m{l}p{tt}"
            self.ln_stats_tile(tt, True, SQ3, ["SQ3a", "SQ3a"], tk + "s")
            rs, nm = self.ln_rows(ROWS3, "L3", tk + "r")
            em("gpsimd", lambda e, tt=tt, rs=rs: e.dma_start(
                out=self.scr_st[l, 0, tt * 512:(tt + 1) * 512], in_=self.rr(rs)),
               reads=["L3rs"], writes=[f"STR{l}_{tt}"], dma=True,
               dmasem=f"st{l}_{tt}", tag=tk + "sr")
            em("gpsimd", lambda e, tt=tt, nm=nm: e.dma_start(
                out=self.scr_st[l, 1, tt * 512:(tt + 1) * 512], in_=self.rr(nm)),
               reads=["L3nm", "L3mu"], writes=[f"STN{l}_{tt}"], dma=True,
               dmasem=f"st{l}_{tt}", tag=tk + "sn")
            self.p.fin_readers(f"st{l}_{tt}", ["L3rs", "L3nm"])
            self.p.fin_group(f"st{l}_{tt}", [f"STR{l}_{tt}", f"STN{l}_{tt}"])

        def rld(tt):
            tk2 = f"m{l}_{tt}"
            sl2 = tt % 2
            em("gpsimd", lambda e, tt=tt, sl2=sl2: e.dma_start(
                out=RST[0:1, sl2 * 512:(sl2 + 1) * 512],
                in_=self.scr_st[l, 0, tt * 512:(tt + 1) * 512]),
               reads=[f"STR{l}_{tt}"], writes=[f"RST{sl2}"], dma=True,
               dmasem=f"rlr{sl2}", tag=tk2 + "lr")
            em("gpsimd", lambda e, tt=tt, sl2=sl2: e.dma_start(
                out=NMU[0:1, sl2 * 512:(sl2 + 1) * 512],
                in_=self.scr_st[l, 1, tt * 512:(tt + 1) * 512]),
               reads=[f"STN{l}_{tt}"], writes=[f"NMU{sl2}"], dma=True,
               dmasem=f"rln{sl2}", tag=tk2 + "ln")
            em("vector", lambda e, sl2=sl2: e.tensor_copy(
                RST[0:1, sl2 * 512:(sl2 + 1) * 512],
                RST[0:1, sl2 * 512:(sl2 + 1) * 512].bitcast(F32)),
               reads=[f"RST{sl2}"], writes=[f"RST{sl2}"], tag=tk2 + "rrR")
            em("vector", lambda e, sl2=sl2: e.tensor_copy(
                NMU[0:1, sl2 * 512:(sl2 + 1) * 512],
                NMU[0:1, sl2 * 512:(sl2 + 1) * 512].bitcast(F32)),
               reads=[f"NMU{sl2}"], writes=[f"NMU{sl2}"], tag=tk2 + "rrM")

        rld(NT - 1)
        for tt in reversed(range(NT)):
            tk = f"m{l}_{tt}"
            sl2 = tt % 2
            em("tensor", lambda e, sl2=sl2: e.matmul(
                PB[1][:, 0:512], self.ONESRR[:, :],
                RST[0:1, sl2 * 512:(sl2 + 1) * 512], start=True, stop=True),
               reads=[f"RST{sl2}", "ONESRR"], writes=["PB1"], tag=tk + "bR")
            em("tensor", lambda e, sl2=sl2: e.matmul(
                PB[0][:, 0:512], self.ONESRR[:, :],
                NMU[0:1, sl2 * 512:(sl2 + 1) * 512], start=True, stop=True),
               reads=[f"NMU{sl2}", "ONESRR"], writes=["PB0"], tag=tk + "bM")
            for cpi in range(3):
                em("scalar", lambda e, l=l, cpi=cpi: e.dma_start(
                    out=W2B[:, cpi * 1024:(cpi + 1) * 1024], in_=self.scr_w2[l, cpi]),
                   reads=[f"SW2_{l}_{cpi}"], writes=[f"W2B{cpi}"], dma=True,
                   dmasem=f"w2b{cpi}", tag=tk + f"zdh{cpi}")
            for dc in range(DC):
                t1 = TMP1[:, 0:512]
                t1k = "TMP1a"
                sh = dc >= 4
                if not sh:
                    src = self.xt(dc, tt * 512, 512); o0 = 0; wd = 512
                elif tt > 0:
                    src = self.xt(dc, tt * 512 - 1, 512); o0 = 0; wd = 512
                else:
                    src = self.xt(dc, 0, 511); o0 = 1; wd = 511
                rkeys = [f"XT{dc}_{tt}"] + ([f"XT{dc}_{tt-1}"] if (sh and tt > 0) else [])
                if o0 == 1:
                    em("vector", lambda e, t1=t1: e.memset(t1[:, 0:1], 0.0),
                       reads=[], writes=[t1k], tag=tk + f"hz{dc}")
                em("vector", lambda e, src=src, o0=o0, wd=wd, t1=t1: e.tensor_tensor(
                    t1[:, o0:o0 + wd], src, PB[1][:, o0:o0 + wd], ALU.mult),
                   reads=rkeys + ["PB1"], writes=[t1k], tag=tk + f"h1{dc}")
                em("vector", lambda e, t1=t1: e.tensor_tensor(
                    t1[:, :], t1[:, :], PB[0][:, 0:512], ALU.add),
                   reads=[t1k, "PB0"], writes=[t1k], tag=tk + f"h2{dc}")
                dst = H2[:, dc * 512:(dc + 1) * 512]
                em("scalar", lambda e, dst=dst, dc=dc, l=l, t1=t1: e.activation(
                    dst, t1[:, :], AF.Identity,
                    bias=self.cp(f"lb{l}_{dc}"), scale=self.cp(f"lg{l}_{dc}")),
                   reads=[t1k, "CP"], writes=[f"H2{dc}"], tag=tk + f"h3{dc}")
            h2k = [f"H2{d}" for d in range(DC)]
            if tt > 0:
                rld(tt - 1)
            for cpi in range(HCH):
                wd = 128 if cpi != HCH - 1 else HLAST
                sl = cpi % 2

                def w1src(cc):
                    base = self.scr_w1[l, cc]
                    if wd == 128:
                        return base
                    return base.rearrange("p (dc m) -> p dc m", m=128)[:, :, 0:wd]

                def w1dst(buf):
                    if wd == 128:
                        return buf[:, sl * 1024:(sl + 1) * 1024]
                    v = buf[:, sl * 1024: sl * 1024 + DC * wd]
                    return v.rearrange("p (dc m) -> p dc m", m=wd)

                em("sync", lambda e, src=w1src(cpi), dst=w1dst(W1A): e.dma_start(
                    out=dst, in_=src),
                   reads=[f"SW1_{l}_{cpi}"], writes=[f"W1A{sl}"], dma=True, dmasem=f"w1a{sl}", tag=tk + f"ad{cpi}")
                em("sync", lambda e, src=w1src(HCH + cpi), dst=w1dst(W1G): e.dma_start(
                    out=dst, in_=src),
                   reads=[f"SW1_{l}_{HCH + cpi}"], writes=[f"W1G{sl}"], dma=True, dmasem=f"w1g{sl}", tag=tk + f"gd{cpi}")
                ab = 2 + sl
                gb = 4 + sl
                for dc in range(DC):
                    opa = [(PB[ab][0:wd, 0:512],
                            W1A[:, sl * 1024 + dc * wd: sl * 1024 + (dc + 1) * wd],
                            H2[:, dc * 512:(dc + 1) * 512], dc == 0, dc == DC - 1)]
                    opg = [(PB[gb][0:wd, 0:512],
                            W1G[:, sl * 1024 + dc * wd: sl * 1024 + (dc + 1) * wd],
                            H2[:, dc * 512:(dc + 1) * 512], dc == 0, dc == DC - 1)]
                    em("tensor", self.mmg_w1a(opa), reads=[f"H2{dc}", f"W1A{sl}"],
                       writes=[f"PB{ab}"], tag=tk + f"am{cpi}_{dc}")
                    em("tensor", self.mmg_w1g(opg), reads=[f"H2{dc}", f"W1G{sl}"],
                       writes=[f"PB{gb}"], tag=tk + f"gm{cpi}_{dc}")
                gel = GEL[:, 0:512]
                em("scalar", lambda e, gel=gel, gb=gb, wd=wd, l=l, cpi=cpi: e.activation(
                    gel[0:wd, :], PB[gb][0:wd, 0:512], AF.Gelu,
                    bias=self.cp(f"b1g{l}_{cpi}", wd)),
                   reads=[f"PB{gb}", "CP"], writes=["GEL0"], tag=tk + f"gg{cpi}")
                dst = U[0:wd, cpi * 512:(cpi + 1) * 512]
                em("vector", lambda e, dst=dst, ab=ab, gel=gel, wd=wd, l=l, cpi=cpi:
                   e.scalar_tensor_tensor(
                       out=dst, in0=PB[ab][0:wd, 0:512], scalar=self.cp(f"b1a{l}_{cpi}", wd),
                       in1=gel[0:wd, :], op0=ALU.add, op1=ALU.mult),
                   reads=[f"PB{ab}", "CP", "GEL0"], writes=[f"U{cpi}"], tag=tk + f"u{cpi}")
            for cpi in range(HCH):
                rows = 128 if cpi != HCH - 1 else HLAST
                sl4 = cpi % 3
                if cpi >= 3:
                    em("scalar", lambda e, l=l, cpi=cpi, sl4=sl4, rows=rows: e.dma_start(
                        out=W2B[0:rows, sl4 * 1024:(sl4 + 1) * 1024], in_=self.scr_w2[l, cpi][0:rows, :]),
                       reads=[f"SW2_{l}_{cpi}"], writes=[f"W2B{sl4}"], dma=True, dmasem=f"w2b{sl4}", tag=tk + f"zd{cpi}")
                ops = []
                for mc in range(DC):
                    ops.append((PB[mc][:, 0:512],
                                W2B[0:rows, sl4 * 1024 + mc * 128: sl4 * 1024 + (mc + 1) * 128],
                                U[0:rows, cpi * 512:(cpi + 1) * 512],
                                cpi == 0, cpi == HCH - 1))
                em("tensor", self.mmg_w2(ops), reads=[f"U{cpi}", f"W2B{sl4}"],
                   writes=[f"PB{m}" for m in range(8)], tag=tk + f"zm{cpi}")
            for mc in range(DC):
                dst = self.xt(mc, tt * 512, 512)
                em("vector", lambda e, dst=dst, mc=mc, l=l: e.scalar_tensor_tensor(
                    out=dst, in0=PB[mc][:, 0:512], scalar=self.cp(f"b2{l}_{mc}"), in1=dst,
                    op0=ALU.add, op1=ALU.add),
                   reads=[f"PB{mc}", "CP", f"XT{mc}_{tt}"], writes=[f"XT{mc}_{tt}"],
                   tag=tk + f"res{mc}")
            if l == 1 and self.upto >= 6:
                for mc in range(DC):
                    sl = mc % 2
                    em("gpsimd", lambda e, tt=tt, mc=mc, sl=sl: e.dma_start(
                        out=X1B[:, sl * 512:(sl + 1) * 512],
                        in_=self.scr_x1[:, mc * SEQ + tt * 512: mc * SEQ + (tt + 1) * 512]),
                       reads=[f"X1S{mc}"], writes=[f"X1B{sl}"], dma=True,
                       dmasem=f"x1l{sl}", tag=tk + f"x1d{mc}")
                    dst = self.xt(mc, tt * 512, 512)
                    em("vector", lambda e, dst=dst, sl=sl: e.tensor_tensor(
                        dst, dst, X1B[:, sl * 512:(sl + 1) * 512], ALU.add),
                       reads=[f"X1B{sl}", f"XT{mc}_{tt}"], writes=[f"XT{mc}_{tt}"],
                       tag=tk + f"x1a{mc}")
        self.free("U", "H2", "W1A", "W1G", "W2B", "GEL", "TMP1", "SQ3",
                  "ROWS3", "X1B", "RST", "NMU")

    def p4(self, final_ln=True):
        em, PB = self.p.emit, self.PB
        self.sbuf("SQ4", [128, 2 * 512], F32R, keys=["SQ4a", "SQ4b"])
        SQ4 = self.SQ4
        self.sbuf("TMP4", [128, 512], F32, keys=["TMP4"])
        TMP4 = self.TMP4
        self.sbuf("ROWS4", [1, 1536], F32, keys=["L4mu", "L4va", "L4sc"])
        ROWSR4 = self.sbuf("ROWSR4", [1, 1024], F32R, keys=["L4rs", "L4nm"])
        ROWS4 = self.ROWS4
        self.sbuf("OSTG", [128, 2 * DM], F32, keys=["OS0", "OS1"])
        OSTG = self.OSTG
        def l4fin(tt):
            b1, b2 = (6, 7) if tt % 2 == 0 else (2, 3)
            rs, nm = self.ln_rows(ROWS4, "L4", f"L4r{tt}_", five=True, ROWSR=ROWSR4,
                                  b1=b1, b2=b2)
            self.ln_bcast(rs, nm, "L4", f"L4b{tt}_")
            for dc in range(DC):
                src = self.xt(dc, tt * 512, 512)
                em("vector", lambda e, src=src: e.tensor_tensor(
                    TMP4[:, :], src, PB[1][:, 0:512], ALU.mult),
                   reads=[f"XT{dc}_{tt}", "PB1"], writes=["TMP4"], tag=f"L4a{tt}_{dc}")
                em("vector", lambda e, src=src: e.tensor_tensor(
                    src, TMP4[:, :], PB[0][:, 0:512], ALU.add),
                   reads=["TMP4", "PB0"], writes=[f"XT{dc}_{tt}"], tag=f"L4b{tt}_{dc}")

        def tstore(tt):
            for tc in range(tt * 4, tt * 4 + 4):
                sl = tc % 2
                st = OSTG[:, sl * DM:(sl + 1) * DM]
                for dc in range(DC):
                    bi = 4 + (dc % 2)
                    em("tensor", lambda e, dc=dc, tc=tc, bi=bi: e.transpose(
                        PB[bi][:, 0:128].bitcast(F32R), self.xt(dc, tc * 128, 128),
                        self.IDENT[:, :]),
                       reads=[f"XT{dc}_{tc // 4}", "IDENT"], writes=[f"PB{bi}"], tag=f"otr{tc}_{dc}")
                    em("scalar", lambda e, st=st, dc=dc, bi=bi: e.copy(
                        st[:, dc * 128:(dc + 1) * 128], PB[bi][:, 0:128]),
                       reads=[f"PB{bi}"], writes=[f"OS{sl}"], tag=f"oev{tc}_{dc}")
                em("sync", lambda e, st=st, tc=tc: e.dma_start(
                    out=self.out[tc * 128:(tc + 1) * 128, :], in_=st),
                   reads=[f"OS{sl}"], writes=[], dma=True, dmasem=f"so{sl}", tag=f"ost{tc}")

        for tt in range(NT if final_ln else 0):
            b1, b2 = (6, 7) if tt % 2 == 0 else (2, 3)
            self.ln_stats_tile(tt, False, SQ4, ["SQ4a", "SQ4b"], f"L4s{tt}_", b1=b1, b2=b2)
            if tt > 0:
                l4fin(tt - 1)
                tstore(tt - 1)
        if final_ln:
            l4fin(NT - 1)
            tstore(NT - 1)
        else:
            for tt in range(NT):
                tstore(tt)
        self.free("SQ4", "TMP4", "ROWS4", "ROWSR4", "OSTG")


def _f32(a):
    return np.ascontiguousarray(np.asarray(a, dtype=np.float32))


def kernel(**inputs):
    consts = host_consts()
    cpak, cidx = pack_cols(inputs)
    mk = MK(cpak.shape[1], cidx)
    nc = mk.build()
    base = {
        "wq": _f32(inputs["L0_attn_wq"]), "wk": _f32(inputs["L0_attn_wk"]),
        "wv": _f32(inputs["L0_attn_wv"]), "wog": _f32(inputs["L0_attn_wo"]),
        "wqkv": _f32(inputs["L0_ffn_wqkv"]), "wol": _f32(inputs["L0_ffn_wo"]),
        "w1": _f32(inputs["L0_ffn_w1"]), "w2": _f32(inputs["L0_ffn_w2"]),
        "CPAK": cpak, "BVB": np.ascontiguousarray(np.tile(_f32(inputs["L0_attn_bv"])[None, :], (128, 1))),
    }
    for nm in ("IDENT", "R128", "ONESC", "ONESR", "ONESCR", "ONESRR", "EBLK",
               "FBLK", "MASKZ", "CSQ", "SSQ", "CSK", "SSK"):
        base[nm] = consts[nm]
    x = _f32(inputs["L0_attn_x"])
    in_maps = [dict(base, x=np.ascontiguousarray(x[c])) for c in range(8)]
    res = run_bass_kernel_spmd(nc, in_maps, core_ids=list(range(8)))
    return np.stack([r["out"] for r in res.results], axis=0)


# version 31
# speedup vs baseline: 1.0150x; 1.0150x over previous
import numpy as np
import concourse.bass as bass
import concourse.mybir as mybir
from concourse.bass_utils import run_bass_kernel_spmd

F32 = mybir.dt.float32
F32R = mybir.dt.float32r
BF16 = mybir.dt.bfloat16
AF = mybir.ActivationFunctionType
ALU = mybir.AluOpType

SEQ = 4096
DM = 1024
DC = 8
NT = 8
WIN = 512
NW = 8
HEADS = 8
DH = 64
INNER = 512
HID = 2730
HCH = 22
HLAST = 42
EPS = 1e-5

ENGS = ("tensor", "vector", "scalar", "gpsimd", "sync")
DMACH = ("sync", "scalar", "gpsimd")


class Prog:
    def __init__(self):
        self.items = {e: [] for e in ENGS}
        self.cnt = {}
        for e in ENGS:
            self.cnt[("E", e)] = 0
        self.dma_sems = set()
        self.res = {}
        self.wm = {e: {} for e in ENGS}
        self.pending_seed = []

    def _wait_list(self, eng, toks):
        merged = {}
        for tok in toks:
            if tok is None:
                continue
            key, val = tok
            if key == ("E", eng) and eng == "tensor":
                continue
            if self.wm[eng].get(key, 0) >= val:
                continue
            merged[key] = max(merged.get(key, 0), val)
        out = []
        for key, val in merged.items():
            out.append((key, val))
            self.wm[eng][key] = val
        return out

    def emit(self, eng, fn, reads=(), writes=(), dma=False, dmasem=None, tag=""):
        toks = []
        for r in reads:
            rec = self.res.get(r)
            if rec and rec["w"]:
                toks.append(rec["w"])
        for w in writes:
            rec = self.res.get(w)
            if rec:
                if rec["w"]:
                    toks.append(rec["w"])
                toks.extend(rec["r"])
        waits = self._wait_list(eng, toks)
        if dma:
            assert dmasem is not None
            key = ("S", dmasem)
            self.dma_sems.add(dmasem)
            self.cnt[key] = self.cnt.get(key, 0) + 16
            tok = (key, self.cnt[key])
            inc_key = key
        else:
            self.cnt[("E", eng)] += 1
            tok = (("E", eng), self.cnt[("E", eng)])
            inc_key = ("E", eng)
        self.items[eng].append((waits, fn, inc_key, tag))
        for w in writes:
            self.res[w] = {"w": tok, "r": []}
        for r in reads:
            rec = self.res.setdefault(r, {"w": None, "r": []})
            rec["r"].append(tok)
        return tok

    def fin_group(self, dmasem, keys):
        total = self.cnt.get(("S", dmasem), 0)
        for k in keys:
            rec = self.res.get(k)
            if rec and rec["w"] and rec["w"][0] == ("S", dmasem):
                rec["w"] = (("S", dmasem), total)

    def fin_readers(self, dmasem, keys):
        total = self.cnt.get(("S", dmasem), 0)
        for k in keys:
            rec = self.res.get(k)
            if rec:
                rec["r"] = [(t if t[0] != ("S", dmasem) else (("S", dmasem), total))
                            for t in rec["r"]]

    def seed(self, key, toks):
        rec = self.res.setdefault(key, {"w": None, "r": []})
        rec["r"] = rec["r"] + [t for t in toks if t is not None]

    def last_tokens(self, keys):
        out = []
        for k in keys:
            rec = self.res.get(k)
            if rec:
                if rec["w"]:
                    out.append(rec["w"])
                out.extend(rec["r"])
        return out

    def replay_one(self, ename, eng, sems):
        for waits, fn, inc_key, tag in self.items[ename]:
            for key, val in waits:
                eng.wait_ge(sems[key], val)
            inst = fn(eng)
            if inst is None:
                raise RuntimeError(f"op '{tag}' on {ename} returned no instruction")
            inst.then_inc(sems[inc_key], 16 if inc_key[0] == "S" else 1)


def host_consts():
    c = {}
    c["IDENT"] = np.eye(128, dtype=np.float32)
    R64 = np.zeros((64, 64), dtype=np.float32)
    for d in range(32):
        R64[d + 32, d] = -1.0
        R64[d, d + 32] = 1.0
    R = np.zeros((128, 128), dtype=np.float32)
    R[:64, :64] = R64
    R[64:, 64:] = R64
    c["R128"] = R
    c["ONESC"] = np.ones((128, 1), dtype=np.float32)
    c["ONESR"] = np.ones((1, 128), dtype=np.float32)
    c["ONESCR"] = np.ones((128, 1), dtype=np.float32)
    c["ONESRR"] = np.ones((1, 128), dtype=np.float32)
    E = np.zeros((128, 4, 8), dtype=np.float32)
    for mc in range(4):
        E[0:64, mc, 2 * mc] = 1.0
        E[64:128, mc, 2 * mc + 1] = 1.0
    c["EBLK"] = np.ascontiguousarray(E.reshape(128, 32))
    Fb = np.zeros((8, 4, 128), dtype=np.float32)
    for mc in range(4):
        Fb[2 * mc, mc, 0:64] = 1.0
        Fb[2 * mc + 1, mc, 64:128] = 1.0
    c["FBLK"] = np.ascontiguousarray(Fb.reshape(8, 512))
    p = np.arange(128)[:, None]
    j = np.arange(128)[None, :]
    P0 = np.where(p > j, 0.0, 1.0)
    P1 = np.where(p < j, 0.0, 1.0)
    Z = np.zeros((128, 384))
    c["MASKZ"] = np.ascontiguousarray(
        np.concatenate([P1, Z, P0], axis=1).astype(np.float32))
    half = np.arange(0, DH, 2, dtype=np.float32)
    inv_freq = (1.0 / 10000.0 ** (half / DH)).astype(np.float32)
    t = np.arange(SEQ, dtype=np.float32)
    freqs = (t[:, None] * inv_freq[None, :]).astype(np.float32)
    freqs = np.concatenate([freqs, freqs], axis=1)
    base = ((half + 0.4 * DH) / (1.4 * DH)).astype(np.float32)
    power = ((t - SEQ // 2) / float(WIN // 2)).astype(np.float32)
    scale = (base[None, :] ** power[:, None]).astype(np.float32)
    scale = np.concatenate([scale, scale], axis=1)
    cos = np.cos(freqs).astype(np.float32)
    sin = np.sin(freqs).astype(np.float32)
    t2 = lambda a: np.ascontiguousarray(np.concatenate([a, a], axis=0).astype(np.float32))
    c["CSQ"] = t2((cos * scale * 8.0).T)
    c["SSQ"] = t2((sin * scale * 8.0).T)
    c["CSK"] = t2((cos / scale).T)
    c["SSK"] = t2((sin / scale).T)
    return c


def pack_cols(inputs):
    cols, idx = [], {}

    def add(name, col):
        v = np.asarray(col, np.float32).reshape(-1)
        if v.size < 128:
            v = np.concatenate([v, np.zeros(128 - v.size, np.float32)])
        idx[name] = len(cols)
        cols.append(v)

    vec128 = lambda v: np.asarray(v, np.float32).reshape(8, 128).T
    bo = vec128(inputs["L0_attn_bo"])
    for dc in range(DC):
        add(f"bo{dc}", bo[:, dc])
    add("bq", inputs["L0_attn_bq"])
    add("bk", inputs["L0_attn_bk"])
    for l in range(2):
        qs = np.asarray(inputs["L0_ffn_q_scale"][l], np.float32)
        ks = np.asarray(inputs["L0_ffn_k_scale"][l], np.float32)
        add(f"qs{l}", np.tile(qs, 2))
        add(f"rqs{l}", np.tile(np.concatenate([qs[32:], qs[:32]]), 2))
        add(f"ks{l}", np.tile(ks, 2))
        add(f"rks{l}", np.tile(np.concatenate([ks[32:], ks[:32]]), 2))
        lg = vec128(inputs["L0_ffn_ln_g"][l])
        lb = vec128(inputs["L0_ffn_ln_b"][l])
        b2 = vec128(inputs["L0_ffn_b2"][l])
        for dc in range(DC):
            add(f"lg{l}_{dc}", lg[:, dc])
            add(f"lb{l}_{dc}", lb[:, dc])
            add(f"b2{l}_{dc}", b2[:, dc])
        b1 = np.asarray(inputs["L0_ffn_b1"][l], np.float32)
        b1a = np.zeros(HCH * 128, np.float32); b1a[:HID] = b1[:HID]
        b1g = np.zeros(HCH * 128, np.float32); b1g[:HID] = b1[HID:]
        b1a = b1a.reshape(HCH, 128).T
        b1g = b1g.reshape(HCH, 128).T
        for cc in range(HCH):
            add(f"b1a{l}_{cc}", b1a[:, cc])
            add(f"b1g{l}_{cc}", b1g[:, cc])
    return np.ascontiguousarray(np.stack(cols, axis=1)), idx


class MK:
    def __init__(self, cpak_width, cpak_idx, upto=6):
        self.upto = upto
        self.nc = bass.Bass("TRN2", target_bir_lowering=False)
        self.p = Prog()
        self.ci = cpak_idx
        self.cpw = cpak_width
        self._sbuf_stack = []
        nc = self.nc
        D = {}
        self.D = D

        def din(name, shape, dtype=F32):
            D[name] = nc.dram_tensor(name, list(shape), dtype, kind="ExternalInput").ap()

        din("x", (SEQ, DM), F32R)
        din("wq", (DM, DH)); din("wk", (DM, DH)); din("wv", (DM, DH)); din("wog", (DH, DM))
        din("wqkv", (2, DM, 3 * INNER))
        din("wol", (2, INNER, DM))
        din("w1", (2, DM, 2 * HID))
        din("w2", (2, HID, DM))
        din("CPAK", (128, cpak_width))
        for nm, shp in [("ONESC", (128, 1)),
                        ("ONESR", (1, 128)),
                        ("MASKZ", (128, 640)), ("CSQ", (128, SEQ)), ("SSQ", (128, SEQ)),
                        ("CSK", (128, SEQ)), ("SSK", (128, SEQ)), ("BVB", (128, DH))]:
            din(nm, shp)
        for nm, shp in [("IDENT", (128, 128)), ("R128", (128, 128)), ("EBLK", (128, 32)),
                        ("FBLK", (8, 512)),
                        ("ONESCR", (128, 1)), ("ONESRR", (1, 128))]:
            din(nm, shp, F32R)
        self.out = nc.dram_tensor("out", [SEQ, DM], F32, kind="ExternalOutput").ap()
        mkint = lambda nm, shp: nc.dram_tensor(nm, list(shp), BF16, kind="Internal").ap()
        self.scr_qkv = mkint("scr_qkv", (2, 8, 128, DC * 128))
        self.scr_v = mkint("scr_v", (2, DC, 128, INNER))
        self.scr_wo = mkint("scr_wo", (2, 4, 128, DM))
        self.scr_w1 = mkint("scr_w1", (2, 2 * HCH, 128, DC * 128))
        self.scr_w2 = mkint("scr_w2", (2, HCH, 128, DM))
        self.scr_x1 = nc.dram_tensor("scr_x1", [128, DC * SEQ], F32, kind="Internal").ap()
        self.scr_st = nc.dram_tensor("scr_st", [2, 2, SEQ], F32R, kind="Internal").ap()

    _sb_uid = 0

    def sbuf(self, name, shape, dtype, keys=None):
        MK._sb_uid += 1
        ctx = self.nc.sbuf_tensor(f"sb{MK._sb_uid}_{name}", list(shape), dtype)
        hdl = ctx.__enter__()
        keys = keys if keys is not None else [name]
        self._sbuf_stack.append((name, ctx, keys))
        for k in keys:
            self.p.seed(k, list(self.p.pending_seed))
        setattr(self, name, hdl)
        return hdl

    def free(self, *names):
        popped = []
        for _ in names:
            popped.append(self._sbuf_stack.pop())
        got = {nm for nm, _, _ in popped}
        assert got == set(names), f"free mismatch: {got} vs {set(names)}"
        for nm, ctx, keys in popped:
            self.p.pending_seed.extend(self.p.last_tokens(keys))
            ctx.__exit__(None, None, None)
            delattr(self, nm)

    def cp(self, name, n=128, p0=0):
        i = self.ci[name]
        return self.CP[p0:p0 + n, i:i + 1]

    def xt(self, dc, c0, n):
        return self.XT[:, dc * SEQ + c0: dc * SEQ + c0 + n]

    @staticmethod
    def r32(ap):
        return ap

    @staticmethod
    def rr(ap):
        return ap.bitcast(F32R)

    def mmg(self, ops):
        def fn(e):
            last = None
            for (out, lhsT, rhs, start, stop) in ops:
                last = e.matmul(out, lhsT, rhs, start=start, stop=stop)
            return last
        return fn

    def mmg_qkv(self, ops):
        def fn(e):
            last = None
            for (out, lhsT, rhs, start, stop) in ops:
                last = e.matmul(out, lhsT, rhs, start=start, stop=stop)
            return last
        return fn

    def mmg_v(self, ops):
        def fn(e):
            last = None
            for (out, lhsT, rhs, start, stop) in ops:
                last = e.matmul(out, lhsT, rhs, start=start, stop=stop)
            return last
        return fn

    def mmg_av(self, ops):
        def fn(e):
            last = None
            for (out, lhsT, rhs, start, stop) in ops:
                last = e.matmul(out, lhsT, rhs, start=start, stop=stop)
            return last
        return fn

    def mmg_wo(self, ops):
        def fn(e):
            last = None
            for (out, lhsT, rhs, start, stop) in ops:
                last = e.matmul(out, lhsT, rhs, start=start, stop=stop)
            return last
        return fn

    def mmg_w1a(self, ops):
        def fn(e):
            last = None
            for (out, lhsT, rhs, start, stop) in ops:
                last = e.matmul(out, lhsT, rhs, start=start, stop=stop)
            return last
        return fn

    def mmg_w1g(self, ops):
        def fn(e):
            last = None
            for (out, lhsT, rhs, start, stop) in ops:
                last = e.matmul(out, lhsT, rhs, start=start, stop=stop)
            return last
        return fn

    def mmg_w2(self, ops):
        def fn(e):
            last = None
            for (out, lhsT, rhs, start, stop) in ops:
                last = e.matmul(out, lhsT, rhs, start=start, stop=stop)
            return last
        return fn

    def build(self):
        em = self.p.emit
        self.sbuf("XT", [128, DC * SEQ], F32R,
                  keys=[f"XT{d}_{t}" for d in range(DC) for t in range(NT)])
        self.sbuf("CP", [128, self.cpw], F32, keys=["CP"])
        self.sbuf("MASKZ", [128, 640], BF16, keys=["MASKZ"])
        self.sbuf("R128", [128, 128], F32R, keys=["R128"])
        self.sbuf("IDENT", [128, 128], F32R, keys=["IDENT"])
        self.sbuf("EBLK", [128, 32], F32R, keys=["EBLK"])
        self.sbuf("FBLK", [8, 512], F32R, keys=["FBLK"])
        self.sbuf("ONESC", [128, 1], F32, keys=["ONESC"])
        self.sbuf("ONESCR", [128, 1], F32R, keys=["ONESCR"])
        self.sbuf("ONESRR", [1, 128], F32R, keys=["ONESRR"])
        self.sbuf("BVB", [128, DH], F32, keys=["BVB"])
        self.sbuf("XSV", [128, 4], F32, keys=["XSV"])
        self.PB = [self.nc.psum_tensor(f"pb{i}", [128, 512], F32).__enter__()
                   for i in range(8)]

        with self.nc.allow_low_precision(reason="f32r structural matmuls"):
            self.const_loads()
            self.p0_weight_casts()
            self.p0_load_x()
            if self.upto >= 1:
                self.p1()
            if self.upto >= 2:
                self.p2(0)
            if self.upto >= 3:
                self.p3(0)
            if self.upto >= 4:
                self.p2(1)
            if self.upto >= 5:
                self.p3(1)
            self.p4(final_ln=(self.upto >= 6))

        fw = [(("S", nm), self.p.cnt.get(("S", nm), 0)) for nm in ("so0", "so1")]
        self.p.items["sync"].append(
            (fw, lambda e: e.nop(), ("E", "sync"), "fence"))
        self.p.cnt[("E", "sync")] += 1

        sems = {}
        for e in ENGS:
            sems[("E", e)] = self.nc.semaphore(f"se_{e}").__enter__()
        for nm in sorted(self.p.dma_sems):
            sems[("S", nm)] = self.nc.semaphore(f"sd_{nm}").__enter__()
        p = self.p
        with self.nc.allow_low_precision(reason="f32r structural matmuls"):
            with self.nc.Block() as block:
                block.tensor(lambda e: p.replay_one("tensor", e, sems))
                block.vector(lambda e: p.replay_one("vector", e, sems))
                block.scalar(lambda e: p.replay_one("scalar", e, sems))
                block.gpsimd(lambda e: p.replay_one("gpsimd", e, sems))
                block.sync(lambda e: p.replay_one("sync", e, sems))
        return self.nc

    def const_loads(self):
        em, D = self.p.emit, self.D
        for nm, dst, key in [("CPAK", self.CP, "CP"),
                             ("MASKZ", self.MASKZ, "MASKZ"),
                             ("R128", self.R128, "R128"), ("IDENT", self.IDENT, "IDENT"),
                             ("EBLK", self.EBLK, "EBLK"), ("FBLK", self.FBLK, "FBLK"),
                             ("ONESC", self.ONESC, "ONESC"),
                             ("ONESCR", self.ONESCR, "ONESCR"), ("ONESRR", self.ONESRR, "ONESRR"),
                             ("BVB", self.BVB, "BVB")]:
            em("gpsimd", lambda e, dst=dst, src=D[nm]: e.dma_start(out=dst[:, :], in_=src),
               writes=[key], dma=True, dmasem="c0", tag=f"ld{nm}")

    def p0_weight_casts(self):
        em, D = self.p.emit, self.D

        def cast(dst, src, key, tag, sem):
            em("gpsimd", lambda e, dst=dst, src=src: e.dma_start(out=dst, in_=src),
               writes=[key], dma=True, dmasem=sem, tag=tag)

        self.sbuf("wqb", [128, DC * DH], BF16, keys=["wqb"])
        self.sbuf("wkb", [128, DC * DH], BF16, keys=["wkb"])
        self.sbuf("wvb", [128, DC * DH], BF16, keys=["wvb"])
        self.sbuf("wogb", [64, DM], BF16, keys=["wogb"])
        for nm, dst, key in [("wq", self.wqb, "wqb"), ("wk", self.wkb, "wkb"),
                             ("wv", self.wvb, "wvb")]:
            src = D[nm].rearrange("(dc p) h -> p dc h", p=128)
            dstv = dst.rearrange("p (dc h) -> p dc h", h=DH)
            cast(dstv, src, key, f"c_{nm}", "c0")
        cast(self.wogb[:, :], D["wog"], "wogb", "c_wog", "c0")
        for l in range(2):
            for mc in range(8):
                src = D["wqkv"][l, :, mc * 128:(mc + 1) * 128].rearrange("(dc p) m -> p dc m", p=128)
                dst = self.scr_qkv[l, mc].rearrange("p (dc m) -> p dc m", m=128)
                cast(dst, src, f"SQKV{l}_{mc}", f"cqkv{l}_{mc}", f"cq{l}")
            for dc in range(DC):
                src = D["wqkv"][l, dc * 128:(dc + 1) * 128, 2 * INNER:3 * INNER]
                cast(self.scr_v[l, dc], src, f"SV{l}_{dc}", f"cv{l}_{dc}", f"cq{l}")
            for kc in range(4):
                src = D["wol"][l, kc * 128:(kc + 1) * 128, :]
                cast(self.scr_wo[l, kc], src, f"SWO{l}_{kc}", f"cwo{l}_{kc}", f"cq{l}")
            for cc in range(2 * HCH):
                w = 128 if (cc % HCH) != HCH - 1 else HLAST
                c0 = (0 if cc < HCH else HID) + (cc % HCH) * 128
                src = D["w1"][l, :, c0:c0 + w].rearrange("(dc p) m -> p dc m", p=128)
                dst = self.scr_w1[l, cc].rearrange("p (dc m) -> p dc m", m=128)[:, :, 0:w]
                cast(dst, src, f"SW1_{l}_{cc}", f"cw1{l}_{cc}", f"ca{l}")
            for cpi in range(HCH):
                rows = 128 if cpi != HCH - 1 else HLAST
                src = D["w2"][l, cpi * 128:cpi * 128 + rows, :]
                cast(self.scr_w2[l, cpi][0:rows, :], src, f"SW2_{l}_{cpi}", f"cw2{l}_{cpi}", f"cb{l}")

        self.p.fin_group("c0", ["CP", "MASKZ", "R128", "IDENT", "EBLK", "FBLK",
                                 "ONESC", "ONESCR", "ONESRR", "BVB",
                                 "wqb", "wkb", "wvb", "wogb"])
        for l in range(2):
            self.p.fin_group(f"cq{l}", [f"SQKV{l}_{m}" for m in range(8)]
                             + [f"SV{l}_{d}" for d in range(DC)]
                             + [f"SWO{l}_{k}" for k in range(4)])
            self.p.fin_group(f"ca{l}", [f"SW1_{l}_{c}" for c in range(2 * HCH)])
            self.p.fin_group(f"cb{l}", [f"SW2_{l}_{c}" for c in range(HCH)])

    def p0_load_x(self):
        em, D, PB = self.p.emit, self.D, self.PB
        xs = self.sbuf("xstage", [128, 4 * DM], F32R, keys=["xs0", "xs1", "xs2", "xs3"])
        for tc in range(32):
            sl = tc % 4
            st = xs[:, sl * DM:(sl + 1) * DM]
            em("sync", lambda e, st=st, tc=tc: e.dma_start(out=st, in_=D["x"][tc * 128:(tc + 1) * 128, :]),
               writes=[f"xs{sl}"], dma=True, dmasem=f"xs{sl}", tag=f"ldx{tc}")
            for dc in range(DC):
                bi = 4 + (dc % 2)
                pb = PB[bi]
                em("tensor", lambda e, st=st, dc=dc, pb=pb:
                   e.transpose(pb[:, 0:128].bitcast(F32R), st[:, dc * 128:(dc + 1) * 128],
                               self.IDENT[:, :]),
                   reads=[f"xs{sl}", "IDENT"], writes=[f"PB{bi}"], tag=f"trx{tc}_{dc}")
                dst = self.xt(dc, tc * 128, 128)
                em("scalar", lambda e, dst=dst, pb=pb: e.copy(dst, pb[:, 0:128]),
                   reads=[f"PB{bi}"], writes=[f"XT{dc}_{tc // 4}"], tag=f"evx{tc}_{dc}")
        self.free("xstage")

    def ln_rows(self, ROWS, rk, tag, five=False, ROWSR=None, b1=6, b2=7):
        em, PB = self.p.emit, self.PB
        mu = ROWS[0:1, 0:512]; va = ROWS[0:1, 512:1024]
        if five:
            rs = ROWSR[0:1, 0:512]; nm = ROWSR[0:1, 512:1024]
            sc = ROWS[0:1, 1024:1536]
            sck = rk + "sc"
        else:
            rs = ROWS[0:1, 1024:1536]; nm = mu
            sc = rs
            sck = rk + "rs"
        em("vector", lambda e: e.tensor_scalar_mul(mu, PB[b1][0:1, 0:512], 1.0 / DM),
           reads=[f"PB{b1}"], writes=[rk + "mu"], tag=tag + "mu")
        em("vector", lambda e, sc=sc: e.tensor_mul(sc, mu, mu),
           reads=[rk + "mu"], writes=[sck], tag=tag + "mu2")
        em("vector", lambda e, sc=sc: e.scalar_tensor_tensor(
            out=va, in0=PB[b2][0:1, 0:512], scalar=1.0 / DM, in1=sc,
            op0=ALU.mult, op1=ALU.subtract),
           reads=[f"PB{b2}", sck], writes=[rk + "va"], tag=tag + "var")
        em("scalar", lambda e, sc=sc: e.activation(sc, va, AF.Ln, bias=np.float32(EPS)),
           reads=[rk + "va"], writes=[sck], tag=tag + "ln")
        rsout = rs if five else self.rr(rs)
        nmout = nm if five else self.rr(nm)
        em("scalar", lambda e, sc=sc, rsout=rsout: e.activation(rsout, sc, AF.Exp, scale=np.float32(-0.5)),
           reads=[sck], writes=[rk + "rs"], tag=tag + "exp")
        em("vector", lambda e, nmout=nmout: e.scalar_tensor_tensor(
            out=nmout, in0=mu, scalar=-1.0, in1=rs, op0=ALU.mult, op1=ALU.mult),
           reads=[rk + "mu", rk + "rs"],
           writes=[rk + "nm"] + ([] if five else [rk + "mu"]), tag=tag + "nmur")
        return rs, nm

    def ln_stats_tile(self, tt, shifted, SQ, sqkeys, tag, b1=6, b2=7):
        em, PB = self.p.emit, self.PB
        for dc in range(DC):
            sh = shifted and dc >= 4
            if not sh:
                rhs = self.xt(dc, tt * 512, 512); o0, owid = 0, 512
            elif tt > 0:
                rhs = self.xt(dc, tt * 512 - 1, 512); o0, owid = 0, 512
            else:
                rhs = self.xt(dc, 0, 511); o0, owid = 1, 511
            rkeys = [f"XT{dc}_{tt}"] + ([f"XT{dc}_{tt-1}"] if (sh and tt > 0) else [])
            if not sh:
                em("tensor", lambda e, rhs=rhs, dc=dc:
                   e.matmul(PB[b1][0:1, 0:512], self.ONESCR[:, :],
                            rhs, start=(dc == 0), stop=(dc == DC - 1)),
                   reads=rkeys + ["ONESCR"], writes=[f"PB{b1}"], tag=f"{tag}s1_{dc}")
            else:
                if owid == 512:
                    em("tensor", lambda e, rhs=rhs, dc=dc:
                       e.matmul(PB[b1][0:1, 0:512], self.ONESCR[:, :],
                                rhs, start=(dc == 0), stop=(dc == DC - 1)),
                       reads=rkeys + ["ONESCR"], writes=[f"PB{b1}"], tag=f"{tag}s1_{dc}")
                else:
                    em("tensor", lambda e, rhs=rhs, o0=o0, owid=owid, dc=dc:
                       e.matmul(PB[b1][0:1, o0:o0 + owid], self.ONESC[:, :],
                                rhs.bitcast(F32), start=(dc == 0), stop=(dc == DC - 1)),
                       reads=rkeys + ["ONESC"], writes=[f"PB{b1}"], tag=f"{tag}s1_{dc}")
            sl = (dc % 2) if len(set(sqkeys)) > 1 else 0
            sq = SQ[:, sl * 512:(sl + 1) * 512]
            em("scalar", lambda e, sq=sq, rhs=rhs, owid=owid, o0=o0:
               e.activation(sq[:, o0:o0 + owid], rhs, AF.Square),
               reads=rkeys, writes=[sqkeys[sl]], tag=f"{tag}sq_{dc}")
            if o0 == 1:
                em("scalar", lambda e, sq=sq, dc=dc: e.activation(
                    sq[:, 0:1], self.xt(dc, 0, 1), AF.Square, scale=np.float32(0.0)),
                   reads=[], writes=[sqkeys[sl]], tag=f"{tag}sqz_{dc}")
            em("tensor", lambda e, sq=sq, dc=dc:
               e.matmul(PB[b2][0:1, 0:512], self.ONESCR[:, :],
                        sq[:, 0:512], start=(dc == 0), stop=(dc == DC - 1)),
               reads=[sqkeys[sl], "ONESCR"], writes=[f"PB{b2}"], tag=f"{tag}s2_{dc}")

    def ln_bcast(self, rs, nm, rk, tag):
        em, PB = self.p.emit, self.PB
        em("tensor", lambda e: e.matmul(PB[1][:, 0:512], self.ONESRR[:, :],
                                        self.rr(rs), start=True, stop=True),
           reads=[rk + "rs", "ONESRR"], writes=["PB1"], tag=tag + "bR")
        em("tensor", lambda e: e.matmul(PB[0][:, 0:512], self.ONESRR[:, :],
                                        self.rr(nm), start=True, stop=True),
           reads=[rk + "nm", "ONESRR"], writes=["PB0"], tag=tag + "bM")

    def p1(self):
        em, D, PB = self.p.emit, self.D, self.PB
        self.sbuf("XB", [128, 8 * 512], BF16, keys=[f"XB{d}" for d in range(DC)])
        XB = self.XB
        self.sbuf("QT", [64, SEQ], BF16, keys=[f"QT{t}" for t in range(NT)])
        QT = self.QT
        self.sbuf("KT", [64, SEQ], BF16, keys=[f"KT{t}" for t in range(NT)])
        KT = self.KT
        self.sbuf("VG", [128, 32 * 65], BF16, keys=[f"VG{tc}" for tc in range(32)] + ["VGones"])
        VG = self.VG
        self.sbuf("AEX", [128, 4 * 512], BF16, keys=["AEX0", "AEX1", "AEX2", "AEX3"])
        AEX = self.AEX
        self.sbuf("OSB", [64, 512], F32, keys=["OSB"])
        OSB = self.OSB
        self.sbuf("OBB", [64, 512], BF16, keys=["OBB"])
        OBB = self.OBB
        self.sbuf("RDEN", [1, 512], F32R, keys=["RDEN"])
        RDEN = self.RDEN
        self.sbuf("BQS", [64, 1], F32, keys=["BQS"])
        BQS = self.BQS
        self.sbuf("SQ1", [128, 2 * 512], F32R, keys=["SQ1a", "SQ1b"])
        SQ1 = self.SQ1
        self.sbuf("TMPA", [128, 512], F32, keys=["TMPA"])
        TMPA = self.TMPA
        self.sbuf("ROWS1", [1, 1536], F32,
                  keys=["L1mu", "L1va", "L1sc"])
        ROWSR1 = self.sbuf("ROWSR1", [1, 1024], F32R, keys=["L1rs", "L1nm"])
        ROWS1 = self.ROWS1

        em("scalar", lambda e: e.mul(BQS[:, :], self.cp("bq", 64), np.float32(0.125)),
           reads=["CP"], writes=["BQS"], tag="bqs")
        vgones = VG.rearrange("p (tc c) -> p tc c", c=65)[:, :, 64:65]
        em("vector", lambda e: e.memset(vgones, 1.0), writes=["VGones"], tag="vgones")

        for qt in range(NT):
            for dc in range(DC):
                em("vector", lambda e, dc=dc, qt=qt: e.tensor_copy(
                    XB[:, dc * 512:(dc + 1) * 512], self.xt(dc, qt * 512, 512)),
                   reads=[f"XT{dc}_{qt}"], writes=[f"XB{dc}"], tag=f"xb{qt}_{dc}")
            xbkeys = [f"XB{d}" for d in range(DC)]
            for (wname, wimg, bank, dstT, bias, scl, dkey) in [
                    ("q", self.wqb, 2, QT, BQS[:, :], 0.125, f"QT{qt}"),
                    ("k", self.wkb, 3, KT, self.cp("bk", 64), 1.0, f"KT{qt}")]:
                ops = []
                for dc in range(DC):
                    ops.append((PB[bank][0:64, 0:512],
                                wimg[:, dc * DH:(dc + 1) * DH],
                                XB[:, dc * 512:(dc + 1) * 512],
                                dc == 0, dc == DC - 1))
                em("tensor", self.mmg(ops), reads=xbkeys + ["wqb", "wkb"],
                   writes=[f"PB{bank}"], tag=f"{wname}mm{qt}")
                dst = dstT[:, qt * 512:(qt + 1) * 512]
                em("scalar", lambda e, dst=dst, bank=bank, bias=bias, scl=scl:
                   e.activation(dst, PB[bank][0:64, 0:512], AF.Identity, bias=bias, scale=np.float32(scl)),
                   reads=[f"PB{bank}", "BQS", "CP"], writes=[dkey], tag=f"{wname}ev{qt}")
            for t4 in range(4):
                bank = (t4 % 2)
                ops = []
                for dc in range(DC):
                    ops.append((PB[bank][:, 0:DH],
                                XB[:, dc * 512 + t4 * 128: dc * 512 + (t4 + 1) * 128],
                                self.wvb[:, dc * DH:(dc + 1) * DH],
                                dc == 0, dc == DC - 1))
                em("tensor", self.mmg(ops), reads=xbkeys + ["wvb"],
                   writes=[f"PB{bank}"], tag=f"vmm{qt}_{t4}")
                tc = qt * 4 + t4
                dst = VG[:, tc * 65: tc * 65 + 64]
                em("vector", lambda e, dst=dst, bank=bank: e.tensor_tensor(
                    dst, PB[bank][:, 0:DH], self.BVB[:, :], ALU.add),
                   reads=[f"PB{bank}", "BVB"], writes=[f"VG{tc}"], tag=f"vev{qt}_{t4}")
            nkc = 4 * qt + 4
            pendav = []

            def gflush(keep=0):
                while len(pendav) > keep:
                    fn = pendav.pop(0); fn()

            for kc in range(nkc):
                par = kc % 4
                sb = [4, 5, 2][kc % 3]
                em("tensor", lambda e, kc=kc, qt=qt, sb=sb: e.matmul(
                    PB[sb][:, 0:512], KT[:, kc * 128:(kc + 1) * 128],
                    QT[:, qt * 512:(qt + 1) * 512], start=True, stop=True),
                   reads=[f"KT{kc // 4}", f"QT{qt}"], writes=[f"PB{sb}"], tag=f"gS{qt}_{kc}")
                em("scalar", lambda e, par=par, sb=sb: e.activation(
                    AEX[:, par * 512:(par + 1) * 512], PB[sb][:, 0:512], AF.Exp),
                   reads=[f"PB{sb}"], writes=[f"AEX{par}"], tag=f"gE{qt}_{kc}")
                r = kc - 4 * qt
                if r >= 0:
                    a = AEX[:, par * 512:(par + 1) * 512]
                    em("vector", lambda e, a=a, r=r: e.tensor_tensor(
                        a[:, 0:(r + 1) * 128], a[:, 0:(r + 1) * 128],
                        self.MASKZ[:, 512 - r * 128:640], ALU.mult),
                       reads=["MASKZ"], writes=[f"AEX{par}"], tag=f"gM{qt}_{kc}")
                gflush(keep=2)
                ob = 6 + (qt % 2)

                def mkgav(kc=kc, par=par, nkc=nkc, ob=ob):
                    def doit():
                        em("tensor", lambda e, kc=kc, par=par, nkc=nkc, ob=ob: e.matmul(
                            PB[ob][0:65, 0:512], VG[:, kc * 65:(kc + 1) * 65],
                            AEX[:, par * 512:(par + 1) * 512],
                            start=(kc == 0), stop=(kc == nkc - 1)),
                           reads=[f"AEX{par}", f"VG{kc}", "VGones"], writes=[f"PB{ob}"],
                           tag=f"gAV{qt}_{kc}")
                    return doit

                pendav.append(mkgav())
            gflush()
            ob = 6 + (qt % 2)
            em("scalar", lambda e, ob=ob: e.activation(
                OSB[0:1, 0:512], PB[ob][64:65, 0:512], AF.Ln),
               reads=[f"PB{ob}"], writes=["OSB"], tag=f"gRa{qt}")
            em("scalar", lambda e: e.activation(
                RDEN[:, :], OSB[0:1, 0:512], AF.Exp, scale=np.float32(-1.0)),
               reads=["OSB"], writes=["RDEN"], tag=f"gR{qt}")
            em("tensor", lambda e: e.matmul(PB[3][0:64, 0:512],
                                            self.ONESRR[:, 0:64],
                                            RDEN[:, :], start=True, stop=True),
               reads=["RDEN", "ONESRR"], writes=["PB3"], tag=f"gB{qt}")
            em("scalar", lambda e, ob=ob: e.copy(OSB[:, :], PB[ob][0:64, 0:512]),
               reads=[f"PB{ob}"], writes=["OSB"], tag=f"gO{qt}")
            em("vector", lambda e: e.tensor_tensor(OBB[:, :], OSB[:, :],
                                                   PB[3][0:64, 0:512], ALU.mult),
               reads=["OSB", "PB3"], writes=["OBB"], tag=f"gN{qt}")
            for mc in range(DC):
                zb = mc % 2
                em("tensor", lambda e, mc=mc, zb=zb: e.matmul(
                    PB[zb][:, 0:512], self.wogb[:, mc * 128:(mc + 1) * 128],
                    OBB[:, :], start=True, stop=True),
                   reads=["OBB", "wogb"], writes=[f"PB{zb}"], tag=f"gW{qt}_{mc}")
                dst = self.xt(mc, qt * 512, 512)
                em("vector", lambda e, dst=dst, zb=zb, mc=mc: e.scalar_tensor_tensor(
                    out=dst, in0=PB[zb][:, 0:512], scalar=self.cp(f"bo{mc}"), in1=dst,
                    op0=ALU.add, op1=ALU.add),
                   reads=[f"PB{zb}", "CP", f"XT{mc}_{qt}"], writes=[f"XT{mc}_{qt}"],
                   tag=f"gres{qt}_{mc}")
        def l1fin(tt):
            b1, b2 = (6, 7) if tt % 2 == 0 else (2, 3)
            rs, nm = self.ln_rows(ROWS1, "L1", f"L1r{tt}_", five=True, ROWSR=ROWSR1,
                                  b1=b1, b2=b2)
            self.ln_bcast(rs, nm, "L1", f"L1b{tt}_")
            for dc in range(DC):
                src = self.xt(dc, tt * 512, 512)
                em("vector", lambda e, src=src: e.tensor_tensor(
                    TMPA[:, :], src, PB[1][:, 0:512], ALU.mult),
                   reads=[f"XT{dc}_{tt}", "PB1"], writes=["TMPA"], tag=f"L1a{tt}_{dc}")
                em("vector", lambda e, src=src: e.tensor_tensor(
                    src, TMPA[:, :], PB[0][:, 0:512], ALU.add),
                   reads=["TMPA", "PB0"], writes=[f"XT{dc}_{tt}"], tag=f"L1b{tt}_{dc}")

        for tt in range(NT):
            b1, b2 = (6, 7) if tt % 2 == 0 else (2, 3)
            self.ln_stats_tile(tt, False, SQ1, ["SQ1a", "SQ1b"], f"L1s{tt}_", b1=b1, b2=b2)
            if tt > 0:
                l1fin(tt - 1)
        l1fin(NT - 1)
        for dc in range(DC):
            em("sync", lambda e, dc=dc: e.dma_start(
                out=self.scr_x1[:, dc * SEQ:(dc + 1) * SEQ],
                in_=self.XT[:, dc * SEQ:(dc + 1) * SEQ].bitcast(F32)),
               reads=[f"XT{dc}_{t}" for t in range(NT)], writes=[f"X1S{dc}"],
               dma=True, dmasem=f"x1s{dc}", tag=f"x1sp{dc}")
        self.free("XB", "QT", "KT", "VG", "AEX", "OSB", "OBB", "RDEN", "BQS",
                  "SQ1", "TMPA", "ROWS1", "ROWSR1")

    def p2(self, l):
        em, D, PB = self.p.emit, self.D, self.PB
        self.sbuf("HT", [128, 8 * 512], BF16, keys=[f"HT{d}" for d in range(DC)])
        HT = self.HT
        self.sbuf("QKVB", [128, 2 * 1024], BF16, keys=["QKVB0", "QKVB1"])
        QKVB = self.QKVB
        self.sbuf("VWB", [128, 2 * 512], BF16, keys=["VWB0", "VWB1"])
        VWB = self.VWB
        self.sbuf("WOSB", [128, 2 * 1024], BF16, keys=["WOSB0", "WOSB1"])
        WOSB = self.WOSB
        self.sbuf("TAB0", [128, 512], F32, keys=["TAB0"])
        TAB0 = self.TAB0
        self.sbuf("TAB1", [128, 512], F32, keys=["TAB1"])
        TAB1 = self.TAB1
        self.sbuf("TAB2", [128, 512], F32, keys=["TAB2"])
        TAB2 = self.TAB2
        self.sbuf("RAW", [128, 4 * 512], F32R, keys=["RAW0", "RAW1", "RAW2", "RAW3"])
        RAW = self.RAW
        self.sbuf("QN", [128, 512], F32R, keys=["QN"])
        QN = self.QN
        self.sbuf("T1", [128, 512], F32, keys=["T1"])
        T1 = self.T1
        self.sbuf("T2", [128, 512], F32R, keys=["T2"])
        T2 = self.T2
        self.sbuf("RNX", [8, 512], F32R, keys=["RNX"])
        RNQ = self.RNX
        RNK = self.RNX
        self.sbuf("RNKT", [128, 64], F32, keys=["RNKT0", "RNKT1"])
        RNKT = self.RNKT
        self.sbuf("Q2B", [128, 4 * 512], BF16, keys=["Q2B0", "Q2B1", "Q2B2", "Q2B3"])
        Q2B = self.Q2B
        self.sbuf("K2B", [128, 2 * 2048], BF16,
                  keys=[f"K2B{sl}_{c}" for sl in "ab" for c in range(4)])
        K2B = self.K2B
        self.sbuf("VL", [128, 2 * 4 * 520], BF16,
                  keys=["VLa", "VLb", "VLones"])
        VL = self.VL
        self.sbuf("OT", [128, 4 * 512], BF16, keys=["OT0", "OT1", "OT2", "OT3"])
        OT = self.OT
        self.sbuf("AEX", [128, 4 * 512], BF16, keys=["AEX0", "AEX1", "AEX2", "AEX3"])
        AEX = self.AEX

        vlones = VL.rearrange("p (s c) -> p s c", c=65)[:, :, 64:65]
        em("vector", lambda e: e.memset(vlones, 1.0), writes=["VLones"], tag=f"vlone{l}")
        for sl in range(4):
            em("vector", lambda e, sl=sl: e.memset(AEX[:, sl * 512:(sl + 1) * 512], 0.0),
               writes=[f"AEX{sl}"], tag=f"aexz{l}_{sl}")

        def emit_ht(w):
            wk = f"w{l}_{w}"
            for dc in range(DC):
                dst = HT[:, dc * 512:(dc + 1) * 512]
                if dc < 4:
                    em("vector", lambda e, dst=dst, dc=dc, w=w: e.tensor_copy(
                        dst, self.xt(dc, w * 512, 512)),
                       reads=[f"XT{dc}_{w}"], writes=[f"HT{dc}"], tag=f"{wk}h{dc}")
                elif w == 0:
                    em("vector", lambda e, dst=dst: e.memset(dst[:, 0:1], 0.0),
                       reads=[], writes=[f"HT{dc}"], tag=f"{wk}h{dc}z")
                    em("vector", lambda e, dst=dst, dc=dc: e.tensor_copy(
                        dst[:, 1:512], self.xt(dc, 0, 511)),
                       reads=[f"XT{dc}_0"], writes=[f"HT{dc}"], tag=f"{wk}h{dc}")
                else:
                    em("vector", lambda e, dst=dst, dc=dc: e.tensor_copy(
                        dst[:, 0:1], self.XSV[:, dc - 4:dc - 3]),
                       reads=["XSV"], writes=[f"HT{dc}"], tag=f"{wk}h{dc}s")
                    em("vector", lambda e, dst=dst, dc=dc, w=w: e.tensor_copy(
                        dst[:, 1:512], self.xt(dc, w * 512, 511)),
                       reads=[f"XT{dc}_{w}"], writes=[f"HT{dc}"], tag=f"{wk}h{dc}")

        def emit_tabs(w):
            wk = f"w{l}_{w}"
            for tb, nm, tbk, sem in [(TAB0, "CSQ", "TAB0", "tb0"),
                                     (TAB1, "SSQ", "TAB1", "tb1"),
                                     (TAB2, "CSK", "TAB2", "tb2")]:
                em("gpsimd", lambda e, tb=tb, nm=nm, w=w: e.dma_start(
                    out=tb[:, :], in_=D[nm][:, w * 512:(w + 1) * 512]),
                   writes=[tbk], dma=True, dmasem=sem, tag=f"{wk}tb{nm}")

        emit_ht(0)
        emit_tabs(0)
        for w in range(NW):
            ks = w % 2
            pks = 1 - ks
            wk = f"w{l}_{w}"
            htk = [f"HT{d}" for d in range(DC)]
            if w < NW - 1:
                for dc in range(4, 8):
                    em("vector", lambda e, dc=dc, w=w: e.tensor_copy(
                        self.XSV[:, dc - 4:dc - 3], self.xt(dc, w * 512 + 511, 1)),
                       reads=[f"XT{dc}_{w}"], writes=["XSV"], tag=f"{wk}xs{dc}")

            pend_n = [None]

            def flushn():
                if pend_n[0] is not None:
                    fn = pend_n[0]
                    pend_n[0] = None
                    fn()

            def proj(mc):
                qsl = mc % 2
                em("sync", lambda e, l=l, mc=mc, qsl=qsl: e.dma_start(
                    out=QKVB[:, qsl * 1024:(qsl + 1) * 1024], in_=self.scr_qkv[l, mc]),
                   reads=[f"SQKV{l}_{mc}"], writes=[f"QKVB{qsl}"], dma=True,
                   dmasem=f"qkvb{qsl}", tag=f"{wk}qkd{mc}")
                bank = [0, 1, 4, 5][mc % 4]
                for dc in range(DC):
                    ops = [(PB[bank][:, 0:512],
                            QKVB[:, qsl * 1024 + dc * 128: qsl * 1024 + (dc + 1) * 128],
                            HT[:, dc * 512:(dc + 1) * 512],
                            dc == 0, dc == DC - 1)]
                    em("tensor", self.mmg_qkv(ops), reads=[f"HT{dc}", f"QKVB{qsl}"],
                       writes=[f"PB{bank}"], tag=f"{wk}qkm{mc}_{dc}")
                flushn()
                ri = mc % 4
                if mc < 4:
                    sqa, sqk = (QN, "QN") if mc % 2 == 0 else (T2, "T2")
                else:
                    sqa, sqk = T2, "T2"
                em("scalar", lambda e, ri=ri, bank=bank: e.copy(
                    RAW[:, ri * 512:(ri + 1) * 512], PB[bank][:, 0:512]),
                   reads=[f"PB{bank}"], writes=[f"RAW{ri}"], tag=f"{wk}qke{mc}")
                em("scalar", lambda e, bank=bank, sqa=sqa: e.activation(
                    sqa[:, 0:512], PB[bank][:, 0:512], AF.Square),
                   reads=[f"PB{bank}"], writes=[sqk], tag=f"{wk}qks{mc}")
                nb = 2 if mc < 4 else 3

                def doit(mc=mc, nb=nb, sqa=sqa, sqk=sqk):
                    em("tensor", lambda e, mc=mc, nb=nb, sqa=sqa: e.matmul(
                        PB[nb][0:8, 0:512],
                        self.EBLK[:, (mc % 4) * 8:(mc % 4) * 8 + 8],
                        sqa[:, 0:512],
                        start=(mc % 4 == 0), stop=(mc % 4 == 3)),
                       reads=[sqk, "EBLK"],
                       writes=[f"PB{nb}"], tag=f"{wk}qkn{mc}")

                pend_n[0] = doit

            for mc in range(4):
                proj(mc)
            flushn()
            em("vector", lambda e: e.tensor_scalar_max(RNQ[:, :], PB[2][0:8, 0:512], 1e-24),
               reads=["PB2"], writes=["RNX"], tag=f"{wk}rnq0")
            em("scalar", lambda e: e.activation(RNQ[:, :], RNQ[:, :], AF.Ln),
               reads=["RNX"], writes=["RNX"], tag=f"{wk}rnq1")
            em("scalar", lambda e: e.activation(RNQ[:, :], RNQ[:, :], AF.Exp, scale=np.float32(-0.5)),
               reads=["RNX"], writes=["RNX"], tag=f"{wk}rnq2")
            for dc in range(DC):
                vs = dc % 2
                em("sync", lambda e, l=l, dc=dc, vs=vs: e.dma_start(
                    out=VWB[:, vs * 512:(vs + 1) * 512], in_=self.scr_v[l, dc]),
                   reads=[f"SV{l}_{dc}"], writes=[f"VWB{vs}"], dma=True,
                   dmasem=f"vwb{vs}", tag=f"{wk}vd{dc}")
                vbanks = [0, 1, 6, 7]
                ops = []
                for t4 in range(4):
                    ops.append((PB[vbanks[t4]][:, 0:512],
                                HT[:, dc * 512 + t4 * 128: dc * 512 + (t4 + 1) * 128],
                                VWB[:, vs * 512:(vs + 1) * 512],
                                dc == 0, dc == DC - 1))
                em("tensor", self.mmg_v(ops), reads=[f"HT{dc}", f"VWB{vs}"],
                   writes=["PB0", "PB1", "PB6", "PB7"], tag=f"{wk}vm{dc}")
            for t4 in range(4):
                vb = [0, 1, 6, 7][t4]
                dst = VL[:, ks * 2080 + t4 * 520: ks * 2080 + (t4 + 1) * 520]
                dstv = dst.rearrange("p (h c) -> p h c", c=65)[:, :, 0:64]
                em("vector", lambda e, dstv=dstv, vb=vb: e.tensor_copy(
                    dstv, PB[vb][:, 0:512].rearrange("p (h c) -> p h c", c=64)),
                   reads=[f"PB{vb}"], writes=[f"VL{'ab'[ks]}"], tag=f"{wk}ve{t4}")
            for qc in range(4):
                rb = [4, 0][qc % 2]
                ro = [5, 1][qc % 2]
                em("tensor", lambda e, qc=qc, rb=rb: e.matmul(
                    PB[rb][:, 0:512], self.FBLK[:, qc * 128:(qc + 1) * 128],
                    RNQ[:, :], start=True, stop=True),
                   reads=["RNX", "FBLK"], writes=[f"PB{rb}"], tag=f"{wk}qb{qc}")
                em("vector", lambda e, qc=qc, rb=rb: e.tensor_tensor(
                    QN[:, :], RAW[:, qc * 512:(qc + 1) * 512],
                    PB[rb][:, 0:512], ALU.mult),
                   reads=[f"RAW{qc}", f"PB{rb}"], writes=["QN"], tag=f"{wk}qn{qc}")
                proj(4 + qc)
                flushn()
                em("tensor", lambda e, ro=ro: e.matmul(
                    PB[ro][:, 0:512], self.R128[:, :],
                    QN[:, :], start=True, stop=True),
                   reads=["QN", "R128"], writes=[f"PB{ro}"], tag=f"{wk}qr{qc}")
                em("vector", lambda e, l=l: e.scalar_tensor_tensor(
                    out=T1[:, :], in0=QN[:, :], scalar=self.cp(f"qs{l}"),
                    in1=TAB0[:, :], op0=ALU.mult, op1=ALU.mult),
                   reads=["QN", "CP", "TAB0"], writes=["T1"], tag=f"{wk}q1{qc}")
                em("vector", lambda e, l=l, ro=ro: e.scalar_tensor_tensor(
                    out=T2[:, :], in0=PB[ro][:, 0:512], scalar=self.cp(f"rqs{l}"),
                    in1=TAB1[:, :], op0=ALU.mult, op1=ALU.mult),
                   reads=[f"PB{ro}", "CP", "TAB1"], writes=["T2"], tag=f"{wk}q2{qc}")
                em("vector", lambda e, qc=qc: e.tensor_tensor(
                    Q2B[:, qc * 512:(qc + 1) * 512], T1[:, :], T2[:, :], ALU.add),
                   reads=["T1", "T2"], writes=[f"Q2B{qc}"], tag=f"{wk}q3{qc}")
            em("gpsimd", lambda e, w=w: e.dma_start(
                out=TAB1[:, :], in_=D["SSK"][:, w * 512:(w + 1) * 512]),
               writes=["TAB1"], dma=True, dmasem="tb1", tag=f"{wk}tbSSK")
            em("vector", lambda e: e.tensor_scalar_max(RNK[:, :], PB[3][0:8, 0:512], 1e-24),
               reads=["PB3"], writes=["RNX"], tag=f"{wk}rnk0")
            em("scalar", lambda e: e.activation(RNK[:, :], RNK[:, :], AF.Ln),
               reads=["RNX"], writes=["RNX"], tag=f"{wk}rnk1")
            em("scalar", lambda e: e.activation(RNK[:, :], RNK[:, :], AF.Exp, scale=np.float32(-0.5)),
               reads=["RNX"], writes=["RNX"], tag=f"{wk}rnk2")
            for sblk in range(4):
                em("tensor", lambda e, sblk=sblk: e.transpose(
                    PB[3][:, 0:8], RNK[:, sblk * 128:(sblk + 1) * 128].bitcast(F32),
                    self.IDENT[0:8, 0:8].bitcast(F32)),
                   reads=["RNX", "IDENT"], writes=["PB3"], tag=f"{wk}rnt{sblk}")
                em("vector", lambda e, sblk=sblk, ks=ks: e.tensor_copy(
                    RNKT[:, ks * 32 + sblk * 8: ks * 32 + (sblk + 1) * 8], PB[3][:, 0:8]),
                   reads=["PB3"], writes=[f"RNKT{ks}"], tag=f"{wk}rne{sblk}")
            for qc in range(4):
                ro = [2, 3][qc % 2]
                em("vector", lambda e, qc=qc, l=l: e.scalar_tensor_tensor(
                    out=T1[:, :], in0=RAW[:, qc * 512:(qc + 1) * 512],
                    scalar=self.cp(f"ks{l}"), in1=TAB2[:, :], op0=ALU.mult, op1=ALU.mult),
                   reads=[f"RAW{qc}", "CP", "TAB2"], writes=["T1"], tag=f"{wk}k1{qc}")
                em("tensor", lambda e, qc=qc, ro=ro: e.matmul(
                    PB[ro][:, 0:512], self.R128[:, :],
                    RAW[:, qc * 512:(qc + 1) * 512], start=True, stop=True),
                   reads=[f"RAW{qc}", "R128"], writes=[f"PB{ro}"], tag=f"{wk}kr{qc}")
                em("vector", lambda e, l=l, ro=ro: e.scalar_tensor_tensor(
                    out=T2[:, :], in0=PB[ro][:, 0:512], scalar=self.cp(f"rks{l}"),
                    in1=TAB1[:, :], op0=ALU.mult, op1=ALU.mult),
                   reads=[f"PB{ro}", "CP", "TAB1"], writes=["T2"], tag=f"{wk}k2{qc}")
                em("vector", lambda e, qc=qc, ks=ks: e.tensor_tensor(
                    K2B[:, ks * 2048 + qc * 512: ks * 2048 + (qc + 1) * 512],
                    T1[:, :], T2[:, :], ALU.add),
                   reads=["T1", "T2"], writes=[f"K2B{'ab'[ks]}_{qc}"], tag=f"{wk}k3{qc}")

            pend = {"av": [], "norm": None}

            def flush_av(keep=0):
                while len(pend["av"]) > keep:
                    fn = pend["av"].pop(0)
                    fn()

            def flush_norm():
                if pend["norm"] is not None:
                    fn = pend["norm"]
                    pend["norm"] = None
                    fn()

            for h in range(HEADS):
                ch = h // 2
                pb0 = 64 * (h % 2)
                kclist = list(range(8)) if w > 0 else list(range(4, 8))
                avwr = 0
                for i, kc in enumerate(kclist):
                    par = i % 4
                    sb = [4, 5, 2][i % 3]
                    qlo = max(0, (kc - 4) * 128)
                    qhi = min(512, kc * 128 + 128)
                    wd = qhi - qlo
                    if kc < 4:
                        ksl = pks
                        kk = f"K2B{'ab'[pks]}_{ch}"
                        vk = f"VL{'ab'[pks]}"
                        rk = f"RNKT{pks}"
                        rsl = pks
                    else:
                        ksl = ks
                        kk = f"K2B{'ab'[ks]}_{ch}"
                        vk = f"VL{'ab'[ks]}"
                        rk = f"RNKT{ks}"
                        rsl = ks
                    lk = K2B[pb0:pb0 + 64,
                             ksl * 2048 + ch * 512 + (kc % 4) * 128:
                             ksl * 2048 + ch * 512 + (kc % 4) * 128 + 128]
                    rq = Q2B[pb0:pb0 + 64, ch * 512 + qlo: ch * 512 + qhi]
                    em("tensor", lambda e, lk=lk, rq=rq, sb=sb, qlo=qlo, wd=wd: e.matmul(
                        PB[sb][:, qlo:qlo + wd], lk, rq, start=True, stop=True),
                       reads=[kk, f"Q2B{ch}"], writes=[f"PB{sb}"], tag=f"{wk}S{h}_{kc}")
                    if i == 3 or (i == 1 and len(kclist) == 4):
                        flush_norm()
                    scl = RNKT[:, rsl * 32 + (kc % 4) * 8 + h: rsl * 32 + (kc % 4) * 8 + h + 1]
                    aex = AEX[:, par * 512:(par + 1) * 512]
                    em("scalar", lambda e, aex=aex, sb=sb, scl=scl, qlo=qlo, wd=wd: e.activation(
                        aex[:, qlo:qlo + wd], PB[sb][:, qlo:qlo + wd], AF.Exp, scale=scl),
                       reads=[f"PB{sb}", rk], writes=[f"AEX{par}"], tag=f"{wk}E{h}_{kc}")
                    if kc >= 4:
                        mq = (kc - 4) * 128
                        em("vector", lambda e, aex=aex, mq=mq: e.tensor_tensor(
                            aex[:, mq:mq + 128], aex[:, mq:mq + 128],
                            self.MASKZ[:, 512:640], ALU.mult),
                           reads=["MASKZ"], writes=[f"AEX{par}"], tag=f"{wk}M{h}_{kc}")
                    else:
                        mq = kc * 128
                        em("vector", lambda e, aex=aex, mq=mq: e.tensor_tensor(
                            aex[:, mq:mq + 128], aex[:, mq:mq + 128],
                            self.MASKZ[:, 0:128], ALU.mult),
                           reads=["MASKZ"], writes=[f"AEX{par}"], tag=f"{wk}M{h}_{kc}")
                    flush_av(keep=2)
                    vt = VL[:, ksl * 2080 + (kc % 4) * 520 + h * 65:
                            ksl * 2080 + (kc % 4) * 520 + (h + 1) * 65]
                    ob = 6 + (h % 2)
                    parts = []
                    if qlo < avwr:
                        parts.append((qlo, min(qhi, avwr)))
                    if qhi > avwr:
                        parts.append((max(qlo, avwr), qhi))
                    avops = []
                    for (p0, p1) in parts:
                        avops.append((PB[ob][0:65, p0:p1], vt,
                                      aex[:, p0:p1],
                                      (i == 0 and p0 == qlo),
                                      (i == len(kclist) - 1 and p1 == qhi)))
                    avwr = max(avwr, qhi)

                    def mkav(avops=avops, par=par, vk=vk, h=h, kc=kc):
                        def doit():
                            em("tensor", self.mmg_av(avops),
                               reads=[f"AEX{par}", vk, "VLones"], writes=[f"PB{6 + (h % 2)}"],
                               tag=f"{wk}AV{h}_{kc}")
                        return doit

                    pend["av"].append(mkav())
                flush_av()
                ob0 = 6 + (h % 2)
                em("scalar", lambda e, ob=ob0: e.activation(
                    RNK[0:1, 0:512], PB[ob][64:65, 0:512], AF.Ln),
                   reads=[f"PB{ob0}"], writes=["RNX"], tag=f"{wk}Ra{h}")
                em("scalar", lambda e: e.activation(
                    T2[0:1, 0:512], RNK[0:1, 0:512], AF.Exp, scale=np.float32(-1.0)),
                   reads=["RNX"], writes=["T2"], tag=f"{wk}R{h}")

                def mknorm(h=h, ch=ch, pb0=pb0):
                    def doit():
                        ob = 6 + (h % 2)
                        em("tensor", lambda e: e.matmul(
                            PB[3][0:64, 0:512], self.ONESRR[:, 0:64],
                            T2[0:1, 0:512], start=True, stop=True),
                           reads=["T2", "ONESRR"], writes=["PB3"], tag=f"{wk}B{h}")
                        em("vector", lambda e, ob=ob: e.tensor_copy(T1[0:64, :], PB[ob][0:64, 0:512]),
                           reads=[f"PB{ob}"], writes=["T1"], tag=f"{wk}O{h}")
                        dst = OT[pb0:pb0 + 64, ch * 512:(ch + 1) * 512]
                        em("vector", lambda e, dst=dst: e.tensor_tensor(
                            dst, T1[0:64, :], PB[3][0:64, 0:512], ALU.mult),
                           reads=["T1", "PB3"], writes=[f"OT{ch}"], tag=f"{wk}N{h}")
                    return doit

                pend["norm"] = mknorm()
            flush_norm()
            if w < NW - 1:
                emit_ht(w + 1)
                emit_tabs(w + 1)
            otk = [f"OT{c}" for c in range(4)]
            for kc4 in range(4):
                wsl = kc4 % 2
                em("gpsimd", lambda e, l=l, kc4=kc4, wsl=wsl: e.dma_start(
                    out=WOSB[:, wsl * 1024:(wsl + 1) * 1024], in_=self.scr_wo[l, kc4]),
                   reads=[f"SWO{l}_{kc4}"], writes=[f"WOSB{wsl}"], dma=True,
                   dmasem=f"wosb{wsl}", tag=f"{wk}wod{kc4}")
                ops = []
                for mc in range(DC):
                    ops.append((PB[mc][:, 0:512],
                                WOSB[:, wsl * 1024 + mc * 128: wsl * 1024 + (mc + 1) * 128],
                                OT[:, kc4 * 512:(kc4 + 1) * 512],
                                kc4 == 0, kc4 == 3))
                em("tensor", self.mmg_wo(ops), reads=otk + [f"WOSB{wsl}"],
                   writes=[f"PB{m}" for m in range(8)], tag=f"{wk}wom{kc4}")
            for mc in range(DC):
                dst = self.xt(mc, w * 512, 512)
                em("vector", lambda e, dst=dst, mc=mc: e.tensor_tensor(
                    dst, PB[mc][:, 0:512], dst, ALU.add),
                   reads=[f"PB{mc}", f"XT{mc}_{w}"], writes=[f"XT{mc}_{w}"],
                   tag=f"{wk}res{mc}")
        self.free("HT", "QKVB", "VWB", "WOSB", "TAB0", "TAB1", "TAB2", "RAW", "QN",
                  "T1", "T2", "RNX", "RNKT", "Q2B", "K2B", "VL",
                  "OT", "AEX")

    def p3(self, l):
        em, PB = self.p.emit, self.PB
        self.sbuf("U", [128, HCH * 512], BF16, keys=[f"U{c}" for c in range(HCH)])
        U = self.U
        self.sbuf("H2", [128, 8 * 512], BF16, keys=[f"H2{d}" for d in range(DC)])
        H2 = self.H2
        self.sbuf("W1A", [128, 2 * 1024], BF16, keys=["W1A0", "W1A1"])
        W1A = self.W1A
        self.sbuf("W1G", [128, 2 * 1024], BF16, keys=["W1G0", "W1G1"])
        W1G = self.W1G
        self.sbuf("W2B", [128, 3 * 1024], BF16, keys=["W2B0", "W2B1", "W2B2"])
        W2B = self.W2B
        self.sbuf("GEL", [128, 512], F32, keys=["GEL0"])
        GEL = self.GEL
        self.sbuf("TMP1", [128, 512], F32, keys=["TMP1a"])
        TMP1 = self.TMP1
        self.sbuf("SQ3", [128, 512], F32R, keys=["SQ3a"])
        SQ3 = self.SQ3
        self.sbuf("ROWS3", [1, 1536], F32, keys=["L3mu", "L3va", "L3rs", "L3nm"])
        ROWS3 = self.ROWS3
        self.sbuf("X1B", [128, 2 * 512], F32, keys=["X1B0", "X1B1"])
        X1B = self.X1B
        self.sbuf("RST", [1, 2 * 512], F32R, keys=["RST0", "RST1"])
        RST = self.RST
        self.sbuf("NMU", [1, 2 * 512], F32R, keys=["NMU0", "NMU1"])
        NMU = self.NMU

        for tt in range(NT):
            tk = f"m{l}p{tt}"
            self.ln_stats_tile(tt, True, SQ3, ["SQ3a", "SQ3a"], tk + "s")
            rs, nm = self.ln_rows(ROWS3, "L3", tk + "r")
            em("gpsimd", lambda e, tt=tt, rs=rs: e.dma_start(
                out=self.scr_st[l, 0, tt * 512:(tt + 1) * 512], in_=self.rr(rs)),
               reads=["L3rs"], writes=[f"STR{l}_{tt}"], dma=True,
               dmasem=f"st{l}_{tt}", tag=tk + "sr")
            em("gpsimd", lambda e, tt=tt, nm=nm: e.dma_start(
                out=self.scr_st[l, 1, tt * 512:(tt + 1) * 512], in_=self.rr(nm)),
               reads=["L3nm", "L3mu"], writes=[f"STN{l}_{tt}"], dma=True,
               dmasem=f"st{l}_{tt}", tag=tk + "sn")
            self.p.fin_readers(f"st{l}_{tt}", ["L3rs", "L3nm"])
            self.p.fin_group(f"st{l}_{tt}", [f"STR{l}_{tt}", f"STN{l}_{tt}"])

        def rld(tt):
            tk2 = f"m{l}_{tt}"
            sl2 = tt % 2
            em("gpsimd", lambda e, tt=tt, sl2=sl2: e.dma_start(
                out=RST[0:1, sl2 * 512:(sl2 + 1) * 512],
                in_=self.scr_st[l, 0, tt * 512:(tt + 1) * 512]),
               reads=[f"STR{l}_{tt}"], writes=[f"RST{sl2}"], dma=True,
               dmasem=f"rlr{sl2}", tag=tk2 + "lr")
            em("gpsimd", lambda e, tt=tt, sl2=sl2: e.dma_start(
                out=NMU[0:1, sl2 * 512:(sl2 + 1) * 512],
                in_=self.scr_st[l, 1, tt * 512:(tt + 1) * 512]),
               reads=[f"STN{l}_{tt}"], writes=[f"NMU{sl2}"], dma=True,
               dmasem=f"rln{sl2}", tag=tk2 + "ln")
            em("vector", lambda e, sl2=sl2: e.tensor_copy(
                RST[0:1, sl2 * 512:(sl2 + 1) * 512],
                RST[0:1, sl2 * 512:(sl2 + 1) * 512].bitcast(F32)),
               reads=[f"RST{sl2}"], writes=[f"RST{sl2}"], tag=tk2 + "rrR")
            em("vector", lambda e, sl2=sl2: e.tensor_copy(
                NMU[0:1, sl2 * 512:(sl2 + 1) * 512],
                NMU[0:1, sl2 * 512:(sl2 + 1) * 512].bitcast(F32)),
               reads=[f"NMU{sl2}"], writes=[f"NMU{sl2}"], tag=tk2 + "rrM")

        rld(NT - 1)
        for tt in reversed(range(NT)):
            tk = f"m{l}_{tt}"
            sl2 = tt % 2
            em("tensor", lambda e, sl2=sl2: e.matmul(
                PB[1][:, 0:512], self.ONESRR[:, :],
                RST[0:1, sl2 * 512:(sl2 + 1) * 512], start=True, stop=True),
               reads=[f"RST{sl2}", "ONESRR"], writes=["PB1"], tag=tk + "bR")
            em("tensor", lambda e, sl2=sl2: e.matmul(
                PB[0][:, 0:512], self.ONESRR[:, :],
                NMU[0:1, sl2 * 512:(sl2 + 1) * 512], start=True, stop=True),
               reads=[f"NMU{sl2}", "ONESRR"], writes=["PB0"], tag=tk + "bM")
            for cpi in range(3):
                em("scalar", lambda e, l=l, cpi=cpi: e.dma_start(
                    out=W2B[:, cpi * 1024:(cpi + 1) * 1024], in_=self.scr_w2[l, cpi]),
                   reads=[f"SW2_{l}_{cpi}"], writes=[f"W2B{cpi}"], dma=True,
                   dmasem=f"w2b{cpi}", tag=tk + f"zdh{cpi}")
            for dc in range(DC):
                t1 = TMP1[:, 0:512]
                t1k = "TMP1a"
                sh = dc >= 4
                if not sh:
                    src = self.xt(dc, tt * 512, 512); o0 = 0; wd = 512
                elif tt > 0:
                    src = self.xt(dc, tt * 512 - 1, 512); o0 = 0; wd = 512
                else:
                    src = self.xt(dc, 0, 511); o0 = 1; wd = 511
                rkeys = [f"XT{dc}_{tt}"] + ([f"XT{dc}_{tt-1}"] if (sh and tt > 0) else [])
                if o0 == 1:
                    em("vector", lambda e, t1=t1: e.memset(t1[:, 0:1], 0.0),
                       reads=[], writes=[t1k], tag=tk + f"hz{dc}")
                em("vector", lambda e, src=src, o0=o0, wd=wd, t1=t1: e.tensor_tensor(
                    t1[:, o0:o0 + wd], src, PB[1][:, o0:o0 + wd], ALU.mult),
                   reads=rkeys + ["PB1"], writes=[t1k], tag=tk + f"h1{dc}")
                em("vector", lambda e, t1=t1: e.tensor_tensor(
                    t1[:, :], t1[:, :], PB[0][:, 0:512], ALU.add),
                   reads=[t1k, "PB0"], writes=[t1k], tag=tk + f"h2{dc}")
                dst = H2[:, dc * 512:(dc + 1) * 512]
                em("scalar", lambda e, dst=dst, dc=dc, l=l, t1=t1: e.activation(
                    dst, t1[:, :], AF.Identity,
                    bias=self.cp(f"lb{l}_{dc}"), scale=self.cp(f"lg{l}_{dc}")),
                   reads=[t1k, "CP"], writes=[f"H2{dc}"], tag=tk + f"h3{dc}")
            h2k = [f"H2{d}" for d in range(DC)]
            if tt > 0:
                rld(tt - 1)
            for cpi in range(HCH):
                wd = 128 if cpi != HCH - 1 else HLAST
                sl = cpi % 2

                def w1src(cc):
                    base = self.scr_w1[l, cc]
                    if wd == 128:
                        return base
                    return base.rearrange("p (dc m) -> p dc m", m=128)[:, :, 0:wd]

                def w1dst(buf):
                    if wd == 128:
                        return buf[:, sl * 1024:(sl + 1) * 1024]
                    v = buf[:, sl * 1024: sl * 1024 + DC * wd]
                    return v.rearrange("p (dc m) -> p dc m", m=wd)

                em("sync", lambda e, src=w1src(cpi), dst=w1dst(W1A): e.dma_start(
                    out=dst, in_=src),
                   reads=[f"SW1_{l}_{cpi}"], writes=[f"W1A{sl}"], dma=True, dmasem=f"w1a{sl}", tag=tk + f"ad{cpi}")
                em("sync", lambda e, src=w1src(HCH + cpi), dst=w1dst(W1G): e.dma_start(
                    out=dst, in_=src),
                   reads=[f"SW1_{l}_{HCH + cpi}"], writes=[f"W1G{sl}"], dma=True, dmasem=f"w1g{sl}", tag=tk + f"gd{cpi}")
                ab = 2 + sl
                gb = 4 + sl
                for dc in range(DC):
                    opa = [(PB[ab][0:wd, 0:512],
                            W1A[:, sl * 1024 + dc * wd: sl * 1024 + (dc + 1) * wd],
                            H2[:, dc * 512:(dc + 1) * 512], dc == 0, dc == DC - 1)]
                    opg = [(PB[gb][0:wd, 0:512],
                            W1G[:, sl * 1024 + dc * wd: sl * 1024 + (dc + 1) * wd],
                            H2[:, dc * 512:(dc + 1) * 512], dc == 0, dc == DC - 1)]
                    em("tensor", self.mmg_w1a(opa), reads=[f"H2{dc}", f"W1A{sl}"],
                       writes=[f"PB{ab}"], tag=tk + f"am{cpi}_{dc}")
                    em("tensor", self.mmg_w1g(opg), reads=[f"H2{dc}", f"W1G{sl}"],
                       writes=[f"PB{gb}"], tag=tk + f"gm{cpi}_{dc}")
                gel = GEL[:, 0:512]
                em("scalar", lambda e, gel=gel, gb=gb, wd=wd, l=l, cpi=cpi: e.activation(
                    gel[0:wd, :], PB[gb][0:wd, 0:512], AF.Gelu,
                    bias=self.cp(f"b1g{l}_{cpi}", wd)),
                   reads=[f"PB{gb}", "CP"], writes=["GEL0"], tag=tk + f"gg{cpi}")
                dst = U[0:wd, cpi * 512:(cpi + 1) * 512]
                em("vector", lambda e, dst=dst, ab=ab, gel=gel, wd=wd, l=l, cpi=cpi:
                   e.scalar_tensor_tensor(
                       out=dst, in0=PB[ab][0:wd, 0:512], scalar=self.cp(f"b1a{l}_{cpi}", wd),
                       in1=gel[0:wd, :], op0=ALU.add, op1=ALU.mult),
                   reads=[f"PB{ab}", "CP", "GEL0"], writes=[f"U{cpi}"], tag=tk + f"u{cpi}")
            for cpi in range(HCH):
                rows = 128 if cpi != HCH - 1 else HLAST
                sl4 = cpi % 3
                if cpi >= 3:
                    em("scalar", lambda e, l=l, cpi=cpi, sl4=sl4, rows=rows: e.dma_start(
                        out=W2B[0:rows, sl4 * 1024:(sl4 + 1) * 1024], in_=self.scr_w2[l, cpi][0:rows, :]),
                       reads=[f"SW2_{l}_{cpi}"], writes=[f"W2B{sl4}"], dma=True, dmasem=f"w2b{sl4}", tag=tk + f"zd{cpi}")
                ops = []
                for mc in range(DC):
                    ops.append((PB[mc][:, 0:512],
                                W2B[0:rows, sl4 * 1024 + mc * 128: sl4 * 1024 + (mc + 1) * 128],
                                U[0:rows, cpi * 512:(cpi + 1) * 512],
                                cpi == 0, cpi == HCH - 1))
                em("tensor", self.mmg_w2(ops), reads=[f"U{cpi}", f"W2B{sl4}"],
                   writes=[f"PB{m}" for m in range(8)], tag=tk + f"zm{cpi}")
            for mc in range(DC):
                dst = self.xt(mc, tt * 512, 512)
                em("vector", lambda e, dst=dst, mc=mc, l=l: e.scalar_tensor_tensor(
                    out=dst, in0=PB[mc][:, 0:512], scalar=self.cp(f"b2{l}_{mc}"), in1=dst,
                    op0=ALU.add, op1=ALU.add),
                   reads=[f"PB{mc}", "CP", f"XT{mc}_{tt}"], writes=[f"XT{mc}_{tt}"],
                   tag=tk + f"res{mc}")
            if l == 1 and self.upto >= 6:
                for mc in range(DC):
                    sl = mc % 2
                    em("gpsimd", lambda e, tt=tt, mc=mc, sl=sl: e.dma_start(
                        out=X1B[:, sl * 512:(sl + 1) * 512],
                        in_=self.scr_x1[:, mc * SEQ + tt * 512: mc * SEQ + (tt + 1) * 512]),
                       reads=[f"X1S{mc}"], writes=[f"X1B{sl}"], dma=True,
                       dmasem=f"x1l{sl}", tag=tk + f"x1d{mc}")
                    dst = self.xt(mc, tt * 512, 512)
                    em("vector", lambda e, dst=dst, sl=sl: e.tensor_tensor(
                        dst, dst, X1B[:, sl * 512:(sl + 1) * 512], ALU.add),
                       reads=[f"X1B{sl}", f"XT{mc}_{tt}"], writes=[f"XT{mc}_{tt}"],
                       tag=tk + f"x1a{mc}")
        self.free("U", "H2", "W1A", "W1G", "W2B", "GEL", "TMP1", "SQ3",
                  "ROWS3", "X1B", "RST", "NMU")

    def p4(self, final_ln=True):
        em, PB = self.p.emit, self.PB
        self.sbuf("SQ4", [128, 2 * 512], F32R, keys=["SQ4a", "SQ4b"])
        SQ4 = self.SQ4
        self.sbuf("TMP4", [128, 512], F32, keys=["TMP4"])
        TMP4 = self.TMP4
        self.sbuf("ROWS4", [1, 1536], F32, keys=["L4mu", "L4va", "L4sc"])
        ROWSR4 = self.sbuf("ROWSR4", [1, 1024], F32R, keys=["L4rs", "L4nm"])
        ROWS4 = self.ROWS4
        self.sbuf("OSTG", [128, 2 * DM], F32, keys=["OS0", "OS1"])
        OSTG = self.OSTG
        def l4fin(tt):
            b1, b2 = (6, 7) if tt % 2 == 0 else (2, 3)
            rs, nm = self.ln_rows(ROWS4, "L4", f"L4r{tt}_", five=True, ROWSR=ROWSR4,
                                  b1=b1, b2=b2)
            self.ln_bcast(rs, nm, "L4", f"L4b{tt}_")
            for dc in range(DC):
                src = self.xt(dc, tt * 512, 512)
                em("vector", lambda e, src=src: e.tensor_tensor(
                    TMP4[:, :], src, PB[1][:, 0:512], ALU.mult),
                   reads=[f"XT{dc}_{tt}", "PB1"], writes=["TMP4"], tag=f"L4a{tt}_{dc}")
                em("vector", lambda e, src=src: e.tensor_tensor(
                    src, TMP4[:, :], PB[0][:, 0:512], ALU.add),
                   reads=["TMP4", "PB0"], writes=[f"XT{dc}_{tt}"], tag=f"L4b{tt}_{dc}")

        def tstore(tt):
            for tc in range(tt * 4, tt * 4 + 4):
                sl = tc % 2
                st = OSTG[:, sl * DM:(sl + 1) * DM]
                for dc in range(DC):
                    bi = 4 + (dc % 2)
                    em("tensor", lambda e, dc=dc, tc=tc, bi=bi: e.transpose(
                        PB[bi][:, 0:128].bitcast(F32R), self.xt(dc, tc * 128, 128),
                        self.IDENT[:, :]),
                       reads=[f"XT{dc}_{tc // 4}", "IDENT"], writes=[f"PB{bi}"], tag=f"otr{tc}_{dc}")
                    em("scalar", lambda e, st=st, dc=dc, bi=bi: e.copy(
                        st[:, dc * 128:(dc + 1) * 128], PB[bi][:, 0:128]),
                       reads=[f"PB{bi}"], writes=[f"OS{sl}"], tag=f"oev{tc}_{dc}")
                em("sync", lambda e, st=st, tc=tc: e.dma_start(
                    out=self.out[tc * 128:(tc + 1) * 128, :], in_=st),
                   reads=[f"OS{sl}"], writes=[], dma=True, dmasem=f"so{sl}", tag=f"ost{tc}")

        for tt in range(NT if final_ln else 0):
            b1, b2 = (6, 7) if tt % 2 == 0 else (2, 3)
            self.ln_stats_tile(tt, False, SQ4, ["SQ4a", "SQ4b"], f"L4s{tt}_", b1=b1, b2=b2)
            if tt > 0:
                l4fin(tt - 1)
                tstore(tt - 1)
        if final_ln:
            l4fin(NT - 1)
            tstore(NT - 1)
        else:
            for tt in range(NT):
                tstore(tt)
        self.free("SQ4", "TMP4", "ROWS4", "ROWSR4", "OSTG")


def _f32(a):
    return np.ascontiguousarray(np.asarray(a, dtype=np.float32))


def kernel(**inputs):
    consts = host_consts()
    cpak, cidx = pack_cols(inputs)
    mk = MK(cpak.shape[1], cidx)
    nc = mk.build()
    base = {
        "wq": _f32(inputs["L0_attn_wq"]), "wk": _f32(inputs["L0_attn_wk"]),
        "wv": _f32(inputs["L0_attn_wv"]), "wog": _f32(inputs["L0_attn_wo"]),
        "wqkv": _f32(inputs["L0_ffn_wqkv"]), "wol": _f32(inputs["L0_ffn_wo"]),
        "w1": _f32(inputs["L0_ffn_w1"]), "w2": _f32(inputs["L0_ffn_w2"]),
        "CPAK": cpak, "BVB": np.ascontiguousarray(np.tile(_f32(inputs["L0_attn_bv"])[None, :], (128, 1))),
    }
    for nm in ("IDENT", "R128", "ONESC", "ONESR", "ONESCR", "ONESRR", "EBLK",
               "FBLK", "MASKZ", "CSQ", "SSQ", "CSK", "SSK"):
        base[nm] = consts[nm]
    x = _f32(inputs["L0_attn_x"])
    in_maps = [dict(base, x=np.ascontiguousarray(x[c])) for c in range(8)]
    res = run_bass_kernel_spmd(nc, in_maps, core_ids=list(range(8)))
    return np.stack([r["out"] for r in res.results], axis=0)
